# version 11 (final)
# baseline (speedup 1.0000x reference)
import math
import numpy as np
from contextlib import ExitStack

import concourse.bass as bass
import concourse.mybir as mybir
from concourse.bass_utils import run_bass_kernel_spmd

F32 = mybir.dt.float32
BF16 = mybir.dt.bfloat16
AF = mybir.ActivationFunctionType
ALU = mybir.AluOpType
AX = mybir.AxisListType

D = 4096
S = 4096
T = 2048
H = 32
DK = 128
DT = D // 128
INT = 8192
IT = INT // 128
LAT = 2048
LT = LAT // 128
NE = 8
EPS = 1e-5
ISQ = 1.0 / math.sqrt(DK)
CB = math.sqrt(128.0)

ENG2BLOCK = {"t": "tensor", "s": "scalar", "v": "vector", "y": "sync", "g": "gpsimd"}
ENGINES = ("t", "s", "v", "y", "g")

DEBUG_TAPS = ()
MAX_PHASE = 99


class Ev:
    __slots__ = ("eng", "fn", "waits", "inc", "cnt")


class Sched:

    def __init__(self, nc, sems, cnt, entry):
        self.nc = nc
        self.sems = sems
        self.cnt = cnt
        self.entry = entry
        self.evs = []
        self._started = set()

    def ev(self, eng, fn, waits=(), inc=True):
        e = Ev()
        e.eng = eng
        e.fn = fn
        ws = []
        if eng not in self._started:
            self._started.add(eng)
            for k in ENGINES:
                if k != eng and self.entry.get(k, 0) > 0:
                    ws.append((k, self.entry[k]))

        def add(w):
            if w is None:
                return
            if isinstance(w, (list, tuple)) and not (len(w) == 2 and isinstance(w[0], str)):
                for w2 in w:
                    add(w2)
            elif isinstance(w, Ev):
                ws.append((w.eng, w.cnt))
            else:
                ws.append(w)
        for w in waits:
            add(w)
        e.waits = ws
        e.inc = inc
        if inc:
            self.cnt[eng] += 16 if eng in ("y", "g") else 1
        e.cnt = self.cnt[eng]
        self.evs.append(e)
        return e

    def emit(self, block):
        per = {k: [] for k in ENGINES}
        for e in self.evs:
            per[e.eng].append(e)
        for eng_key in ENGINES:
            evs = per[eng_key]
            if not evs:
                continue

            def body(engine, evs=evs, eng_key=eng_key, hw={}):
                for e in evs:
                    for (weng, wval) in e.waits:
                        if wval > hw.get(weng, 0):
                            engine.wait_ge(self.sems[weng], wval)
                            hw[weng] = wval
                    instr = e.fn(engine)
                    if e.inc:
                        instr.then_inc(self.sems[eng_key], 16 if eng_key in ("y", "g") else 1)

            getattr(block, ENG2BLOCK[eng_key])(body)


def build_program():
    nc = bass.Bass("TRN2", target_bir_lowering=False)

    def dram(name, shape, dtype, kind):
        if kind == "Internal" and name in DEBUG_TAPS:
            kind = "ExternalOutput"
        return nc.dram_tensor(name, shape, dtype, kind=kind).ap()

    x_all = dram("x_all", [S, D], F32, "ExternalInput")
    wq_p = dram("wq_p", [DT, 128, D], F32, "ExternalInput")
    wk_p = dram("wk_p", [DT, 128, D], F32, "ExternalInput")
    wvf_p = dram("wvf_p", [DT, 128, D], F32, "ExternalInput")
    wo_p = dram("wo_p", [DT, 128, D], F32, "ExternalInput")
    up_p = dram("up_p", [IT, 128, D], F32, "ExternalInput")
    down_p = dram("down_p", [DT, 128, INT], F32, "ExternalInput")
    fc1_p = dram("fc1_p", [LT, 128, D], F32, "ExternalInput")
    fc2_p = dram("fc2_p", [DT, 128, LAT], F32, "ExternalInput")
    exp_p = dram("exp_p", [LT, 128, LAT], F32, "ExternalInput")
    gate_h = dram("gate_h", [128, DT * NE], F32, "ExternalInput")
    gamma_h = dram("gamma_h", [128, D], F32, "ExternalInput")
    beta_h = dram("beta_h", [128, D], F32, "ExternalInput")
    eye_h = dram("eye_h", [128, 128], F32, "ExternalInput")
    ones_col_h = dram("ones_col_h", [128, 1], F32, "ExternalInput")
    eps_h = dram("eps_h", [128, 1], F32, "ExternalInput")
    ones_row_h = dram("ones_row_h", [1, 128], F32, "ExternalInput")

    y_out = dram("y_out", [T, D], F32, "ExternalOutput")

    qTs = dram("qTs", [D, T], BF16, "Internal")
    kTs = dram("kTs", [D, S], BF16, "Internal")
    vTs_tok = dram("vTs_tok", [S, D], BF16, "Internal")
    attnTs = dram("attnTs", [D, T], BF16, "Internal")
    x2s = dram("x2s", [T, D], F32, "Internal")
    ln2s = dram("ln2s", [T, D], BF16, "Internal")
    hTs = dram("hTs", [INT, T], BF16, "Internal")
    routedTs = dram("routedTs", [D, T], F32, "Internal")
    fpreTs = dram("fpreTs", [D, T], BF16, "Internal")

    sems = {}
    cnt = {k: 0 for k in ENGINES}

    with ExitStack() as top:
        E = top.enter_context
        for k in ENGINES:
            sems[k] = E(nc.semaphore("sem_" + k))

        eye_bf = E(nc.sbuf_tensor("eye_bf", [128, 128], BF16))
        eye_f = E(nc.sbuf_tensor("eye_f", [128, 128], F32))
        ones_qn = E(nc.sbuf_tensor("ones_qn", [128, 1], BF16))
        ones_row = E(nc.sbuf_tensor("ones_row", [1, 128], F32))
        eps_sb = E(nc.sbuf_tensor("eps_sb", [128, 1], F32))
        gate_sb = E(nc.sbuf_tensor("gate_sb", [128, DT * NE], BF16))

        p_s0 = E(nc.psum_tensor("p_s0", [128, 1024], F32))
        p_s1 = E(nc.psum_tensor("p_s1", [128, 1024], F32))

        block = E(nc.Block())
        nums = sorted(sems[k].num for k in ENGINES)
        assert nums[-1] - nums[0] == len(ENGINES) - 1
        nc.gpsimd.dma_reset(range(nums[0], nums[-1] + 1))
        nc.gpsimd.sem_clear(range(nums[0], nums[-1] + 1))
        nc.all_engine_barrier()

        snap = dict(cnt)

        def new_sched():
            return Sched(nc, sems, cnt, dict(snap))

        def finish(sch):
            nonlocal snap
            sch.emit(block)
            snap = dict(cnt)

        sch = new_sched()
        sch.ev("g", lambda e: e.dma_start(eye_bf[:], eye_h))
        sch.ev("g", lambda e: e.dma_start(ones_qn[:], ones_col_h))
        sch.ev("g", lambda e: e.dma_start(gate_sb[:], gate_h))
        sch.ev("y", lambda e: e.dma_start(eye_f[:], eye_h))
        sch.ev("y", lambda e: e.dma_start(ones_row[:], ones_row_h))
        sch.ev("y", lambda e: e.dma_start(eps_sb[:], eps_h))
        finish(sch)

        lastuse = {}

        def transpose_tile(sch, src_tile_fn, n_dt, dst_fn, waits_src, banks):
            evs = []
            for b8 in range(n_dt // 8):
                bk = banks[b8 % len(banks)]

                def tf(e, b8=b8, bk=bk):
                    last = None
                    for j in range(8):
                        last = e.transpose(bk[:, j * 128:(j + 1) * 128],
                                           src_tile_fn(b8 * 8 + j), eye_bf[:])
                    return last
                tev = sch.ev("t", tf, waits=list(waits_src) + [lastuse.get(("p_tr", id(bk)))])
                vev = sch.ev("v", (lambda e, b8=b8, bk=bk: e.tensor_copy(
                    dst_fn(b8), bk[:].rearrange("p (j q) -> p j q", j=8))),
                             waits=[tev])
                lastuse[("p_tr", id(bk))] = vev
                evs.append(vev)
            return evs

        def ln_stats(sch, src_fn, bnb, mv, st1, waits, pkey="statprev"):
            def stats(e):
                last = None
                for j in range(8):
                    last = e.bn_stats(bnb[:, j * 6:(j + 1) * 6], src_fn(j))
                return last
            v1a = sch.ev("v", stats, waits=list(waits))
            v1b = sch.ev("v", lambda e: e.bn_aggr(mv[:], bnb[:]), waits=[v1a])
            s1 = sch.ev("s", lambda e: e.activation(st1[:, 1:2], mv[:, 1:2], AF.Sqrt,
                                                    bias=eps_sb[:, 0:1]),
                        waits=[v1b])
            v2a = sch.ev("v", lambda e: e.reciprocal(st1[:, 2:3], st1[:, 1:2]),
                         waits=[s1, lastuse.get(pkey)])
            v2b = sch.ev("v", (lambda e: e.tensor_scalar(st1[:, 3:4], mv[:, 0:1], st1[:, 2:3],
                                                         -1.0, ALU.mult, ALU.mult)), waits=[v2a])
            return v2b

        if MAX_PHASE >= 1:
            with ExitStack() as ph:
                A = ph.enter_context
                ln1T = A(nc.sbuf_tensor("ln1T", [128, DT * 2048], BF16))
                p_trA = [A(nc.psum_tensor(f"p_trA{i}", [128, 1024], BF16)) for i in range(2)]

                ln1T_r = ln1T[:].rearrange("p (dt t) -> p dt t", dt=DT)

                for c in range(2):
                    with ExitStack() as lsc:
                        L = lsc.enter_context
                        xa = [L(nc.sbuf_tensor(f"xa{c}_{i}", [128, D], F32)) for i in range(2)]
                        ln1_sb = [L(nc.sbuf_tensor(f"ln1_{c}_{i}", [128, D], BF16)) for i in range(2)]
                        bnb = [L(nc.sbuf_tensor(f"bnbA{c}_{i}", [128, 48], F32)) for i in range(2)]
                        mv = [L(nc.sbuf_tensor(f"mvA{c}_{i}", [128, 2], F32)) for i in range(2)]
                        st1 = [L(nc.sbuf_tensor(f"st1A{c}_{i}", [128, 4], F32)) for i in range(2)]

                        sch = new_sched()
                        for tp in range(8):
                            lds, v2s, s2s = [], [], []
                            for q in range(2):
                                tt = tp * 2 + q
                                row0 = c * 2048 + tt * 128
                                ld = sch.ev("y", (lambda e, row0=row0, q=q:
                                                  e.dma_start(xa[q][:], x_all[row0:row0 + 128, :])),
                                            waits=[lastuse.get(("xa", q))])
                                lds.append(ld)
                            for q in range(2):
                                v2s.append(ln_stats(
                                    sch, (lambda j, q=q: xa[q][:, j * 512:(j + 1) * 512]),
                                    bnb[q], mv[q], st1[q], [lds[q]], pkey=("spA", q)))
                            for q in range(2):
                                s2 = sch.ev("s", (lambda e, q=q:
                                                  e.activation(ln1_sb[q][:], xa[q][:], AF.Identity,
                                                               bias=st1[q][:, 3:4],
                                                               scale=st1[q][:, 2:3])),
                                            waits=[v2s[q], lastuse.get(("ln1_sb", q))])
                                lastuse[("spA", q)] = s2
                                lastuse[("xa", q)] = s2
                                s2s.append(s2)
                            for q in range(2):
                                tt = tp * 2 + q
                                tevs = transpose_tile(
                                    sch,
                                    (lambda dt, q=q: ln1_sb[q][:, dt * 128:(dt + 1) * 128]),
                                    DT,
                                    (lambda b8, tt=tt: ln1T_r[:, b8 * 8:(b8 + 1) * 8,
                                                              tt * 128:(tt + 1) * 128]),
                                    [s2s[q]],
                                    p_trA)
                                lastuse[("ln1_sb", q)] = tevs[-1]
                        finish(sch)

                    with ExitStack() as qsc:
                        Q = qsc.enter_context
                        wslab = [Q(nc.sbuf_tensor(f"wslabA{c}_{i}", [128, D], BF16)) for i in range(2)]
                        evac = [Q(nc.sbuf_tensor(f"evacA{c}_{i}", [128, 1024], BF16)) for i in range(2)]

                        sch = new_sched()
                        passes = []
                        if c == 0:
                            passes.append((wq_p, qTs, ISQ, 0))
                        passes.append((wk_p, kTs, 1.0, c * 2048))

                        for (planes, spill, scale, colbase) in passes:
                            for mt in range(DT):
                                sl = mt % 2
                                g1 = sch.ev("g", (lambda e, planes=planes, mt=mt, sl=sl:
                                                  e.dma_start(wslab[sl][:], planes[mt, :, :])),
                                            waits=[lastuse.get(("wslabA", sl))])
                                for th in range(2):
                                    pp = (p_s0, p_s1)[(mt * 2 + th) % 2]

                                    def mm(e, pp=pp, sl=sl, th=th):
                                        last = None
                                        for kt in range(DT):
                                            lhs = wslab[sl][:, kt * 128:(kt + 1) * 128]
                                            for j in range(2):
                                                rhs = ln1T[:, kt * 2048 + th * 1024 + j * 512:
                                                           kt * 2048 + th * 1024 + j * 512 + 512]
                                                last = e.matmul(pp[:, j * 512:(j + 1) * 512], lhs, rhs,
                                                                start=(kt == 0), stop=(kt == DT - 1))
                                        return last
                                    me = sch.ev("t", mm, waits=[g1, lastuse.get(("ppA", id(pp)))])
                                    lastuse[("wslabA", sl)] = me
                                    esl = (mt * 2 + th) % 2
                                    se = sch.ev("s", (lambda e, pp=pp, esl=esl, scale=scale:
                                                      e.activation(evac[esl][:], pp[:], AF.Copy, scale=scale)),
                                                waits=[me, lastuse.get(("evacA", esl))])
                                    lastuse[("ppA", id(pp))] = se
                                    ye = sch.ev("y", (lambda e, spill=spill, mt=mt, th=th, esl=esl, colbase=colbase:
                                                      e.dma_start(spill[mt * 128:(mt + 1) * 128,
                                                                  colbase + th * 1024: colbase + th * 1024 + 1024],
                                                                  evac[esl][:])),
                                                waits=[se])
                                    lastuse[("evacA", esl)] = ye
                        finish(sch)

                    with ExitStack() as vsc:
                        V = vsc.enter_context
                        wvslab = [V(nc.sbuf_tensor(f"wvslab{c}_{i}", [128, DT * 512], BF16))
                                  for i in range(2)]
                        vsb = [V(nc.sbuf_tensor(f"vsb{c}_{i}", [128, 512], BF16)) for i in range(2)]

                        sch = new_sched()
                        for dc in range(8):
                            sl = dc % 2
                            g1 = sch.ev("g", (lambda e, dc=dc, sl=sl:
                                              e.dma_start(
                                                  wvslab[sl][:].rearrange("p (kt q) -> p kt q", kt=DT),
                                                  wvf_p[:, :, dc * 512:(dc + 1) * 512]
                                                  .rearrange("kt p q -> p kt q"))),
                                        waits=[lastuse.get(("wvslab", sl))])
                            last_mm = None
                            for tt in range(16):
                                slot = (dc * 16 + tt) % 4
                                pp = (p_s0, p_s1)[slot // 2]
                                col = (slot % 2) * 512

                                def mm(e, pp=pp, col=col, sl=sl, tt=tt):
                                    last = None
                                    for kt in range(DT):
                                        last = e.matmul(pp[:, col:col + 512],
                                                        ln1T[:, kt * 2048 + tt * 128:kt * 2048 + tt * 128 + 128],
                                                        wvslab[sl][:, kt * 512:(kt + 1) * 512],
                                                        start=(kt == 0), stop=(kt == DT - 1))
                                    return last
                                me = sch.ev("t", mm, waits=[g1, lastuse.get(("slotV", slot))])
                                last_mm = me
                                esl = (dc * 16 + tt) % 2
                                eng = "s" if (tt % 2 == 0) else "v"
                                if eng == "s":
                                    ee = sch.ev("s", (lambda e, pp=pp, col=col, esl=esl:
                                                      e.activation(vsb[esl][:], pp[:, col:col + 512], AF.Copy)),
                                                waits=[me, lastuse.get(("vsb", esl))])
                                else:
                                    ee = sch.ev("v", (lambda e, pp=pp, col=col, esl=esl:
                                                      e.tensor_copy(vsb[esl][:], pp[:, col:col + 512])),
                                                waits=[me, lastuse.get(("vsb", esl))])
                                lastuse[("slotV", slot)] = ee
                                ye = sch.ev("y", (lambda e, c=c, tt=tt, dc=dc, esl=esl:
                                                  e.dma_start(vTs_tok[c * 2048 + tt * 128:c * 2048 + tt * 128 + 128,
                                                              dc * 512:(dc + 1) * 512],
                                                              vsb[esl][:])),
                                            waits=[ee])
                                lastuse[("vsb", esl)] = ye
                            lastuse[("wvslab", sl)] = last_mm
                        finish(sch)

        if MAX_PHASE >= 2:
            with ExitStack() as ph:
                A = ph.enter_context
                kT = [A(nc.sbuf_tensor(f"kT{i}", [128, S], BF16)) for i in range(2)]
                qT = [A(nc.sbuf_tensor(f"qT{i}", [128, T], BF16)) for i in range(2)]
                vt4 = [A(nc.sbuf_tensor(f"vt4_{i}", [128, 32 * 512], BF16)) for i in range(2)]
                PT = [A(nc.sbuf_tensor(f"PT{i}", [128, 16 * 1024], BF16)) for i in range(2)]
                sqk = A(nc.sbuf_tensor("sqk", [128, S], BF16))
                sqq = A(nc.sbuf_tensor("sqq", [128, T], BF16))
                kn2 = A(nc.sbuf_tensor("kn2", [1, S], BF16))
                qn2 = A(nc.sbuf_tensor("qn2", [1, T], BF16))
                kst = A(nc.sbuf_tensor("kst", [1, 8], F32))
                nbh = A(nc.sbuf_tensor("nbh", [128, 2], F32))
                rrow = A(nc.sbuf_tensor("rrow", [1, 2048], F32))
                fct = A(nc.sbuf_tensor("fct", [128, 2048], F32))
                attn_sb = [A(nc.sbuf_tensor(f"attn{i}", [128, 1024], BF16)) for i in range(2)]

                p_pv = A(nc.psum_tensor("p_pv", [128, 1024], F32))
                p_d = A(nc.psum_tensor("p_d", [128, 512], F32))
                p_f = A(nc.psum_tensor("p_f", [128, 512], F32))

                sch = new_sched()
                psi = [0]
                exps_all = {}

                def loads(h):
                    sl = h % 2
                    lK = sch.ev("y", (lambda e, h=h, sl=sl:
                                      e.dma_start(kT[sl][:], kTs[h * 128:(h + 1) * 128, :])),
                                waits=[lastuse.get(("kT", sl))])
                    lQ = sch.ev("y", (lambda e, h=h, sl=sl:
                                      e.dma_start(qT[sl][:], qTs[h * 128:(h + 1) * 128, :])),
                                waits=[lastuse.get(("qT", sl))])
                    lastuse[("ld", h)] = (lK, lQ)

                def loadv(hg):
                    sl = hg % 2
                    lV = sch.ev("g", (lambda e, hg=hg, sl=sl:
                                      e.dma_start(
                                          vt4[sl][:].rearrange("p (kt q) -> p kt q", kt=32),
                                          vTs_tok.rearrange("(kt p) d -> p kt d", p=128)
                                          [:, :, hg * 512:(hg + 1) * 512])),
                                waits=[lastuse.get(("vt4", sl))])
                    lastuse[("lv", hg)] = lV

                def b_sq(h):
                    sl = h % 2
                    lK, lQ = lastuse[("ld", h)]
                    sq1 = sch.ev("v", (lambda e, sl=sl: e.tensor_mul(sqk[:], kT[sl][:], kT[sl][:])),
                                 waits=[lK, lastuse.get("sqk_rd")])
                    sq2 = sch.ev("v", (lambda e, sl=sl: e.tensor_mul(sqq[:], qT[sl][:], qT[sl][:])),
                                 waits=[lQ, lastuse.get("sqq_rd")])
                    lastuse[("bsq", h)] = (sq1, sq2)

                def b_mm(h, j):
                    sq1, sq2 = lastuse[("bsq", h)]
                    if j < 8:
                        src, dst, sqe = sqk, kn2, sq1
                        jj = j
                    else:
                        src, dst, sqe = sqq, qn2, sq2
                        jj = j - 8
                    me = sch.ev("t", (lambda e, src=src, jj=jj:
                                      e.matmul(p_f[0:1, 0:512], ones_qn[:],
                                               src[:, jj * 512:(jj + 1) * 512], start=True, stop=True)),
                                waits=[sqe, lastuse.get(("pb", 1))])
                    ce = sch.ev("v", (lambda e, dst=dst, jj=jj:
                                      e.tensor_copy(dst[0:1, jj * 512:(jj + 1) * 512], p_f[0:1, 0:512])),
                                waits=[me])
                    lastuse[("pb", 1)] = ce
                    lastuse[("bcp", h, j)] = ce
                    if j == 7:
                        lastuse["sqk_rd"] = me
                    if j == 11:
                        lastuse["sqq_rd"] = me

                def b_fin(h):
                    kevs = [lastuse[("bcp", h, j)] for j in range(8)]
                    qevs = [lastuse[("bcp", h, j)] for j in range(8, 12)]
                    v1 = sch.ev("v", lambda e: e.reduce_max(kst[0:1, 0:1], kn2[0:1, :], axis=AX.X),
                                waits=kevs + [lastuse.get("kst_rd")])
                    v2 = sch.ev("v", lambda e: e.reduce_max(kst[0:1, 1:2], qn2[0:1, :], axis=AX.X),
                                waits=qevs)
                    v3 = sch.ev("v", (lambda e: e.tensor_scalar(kst[0:1, 2:3], kst[0:1, 0:1],
                                                                -0.5 / CB, None, ALU.mult)),
                                waits=[v1])
                    v4 = sch.ev("v", (lambda e: e.tensor_scalar(kst[0:1, 3:4], kst[0:1, 1:2],
                                                                -0.5 * CB, kst[0:1, 2:3],
                                                                ALU.mult, ALU.add)),
                                waits=[v2, v3])
                    tb = sch.ev("t", (lambda e: e.matmul(p_f[:, 0:1], ones_row[:], kst[0:1, 3:4],
                                                         start=True, stop=True)),
                                waits=[v4, lastuse.get(("pb", 1))])
                    lastuse["kst_rd"] = tb
                    hp = h % 2
                    nb = sch.ev("v", (lambda e, hp=hp: e.tensor_copy(nbh[:, hp:hp + 1], p_f[:, 0:1])),
                                waits=[tb, lastuse.get(("nbh_rd", hp))])
                    lastuse[("pb", 1)] = nb
                    lastuse[("nbh", h)] = nb

                def bounds_fillers(h):
                    return ([lambda h=h: b_sq(h)] +
                            [lambda h=h, j=j: b_mm(h, j) for j in range(12)] +
                            [lambda h=h: b_fin(h)])

                def qk_one(h, qh, kth, ktl):
                    sl = h % 2
                    lK, lQ = lastuse[("ld", h)]
                    nbev = lastuse[("nbh", h)]
                    hp = h % 2
                    kt = kth * 16 + ktl
                    pidx = psi[0] % 2
                    psi[0] += 1
                    pp = (p_s0, p_s1)[pidx]

                    def mmf(e, pp=pp, sl=sl, kt=kt, qh=qh):
                        last = None
                        for qc in range(2):
                            last = e.matmul(pp[:, qc * 512:(qc + 1) * 512],
                                            kT[sl][:, kt * 128:(kt + 1) * 128],
                                            qT[sl][:, qh * 1024 + qc * 512:qh * 1024 + qc * 512 + 512],
                                            start=True, stop=True)
                        return last
                    me = sch.ev("t", mmf, waits=[lK, lQ, lastuse.get(("psB", pidx))])
                    se = sch.ev("s", (lambda e, pp=pp, kth=kth, ktl=ktl, hp=hp:
                                      e.activation(PT[kth][:, ktl * 1024:(ktl + 1) * 1024],
                                                   pp[:], AF.Exp, bias=nbh[:, hp:hp + 1])),
                                waits=[me, lastuse.get(("PTrd", kth)), nbev])
                    lastuse[("psB", pidx)] = se
                    lastuse[("kT", sl)] = me
                    lastuse[("qT", sl)] = me
                    lastuse[("nbh_rd", hp)] = se
                    exps_all[(h, qh, kth)].append(se)

                def pv_one(h, qh, kth, ktl):
                    sl4 = (h // 4) % 2
                    lV = lastuse[("lv", h // 4)]

                    def pm(e, kth=kth, ktl=ktl, sl4=sl4, h=h):
                        last = None
                        lhs = vt4[sl4][:, (kth * 16 + ktl) * 512 + (h % 4) * 128:
                                       (kth * 16 + ktl) * 512 + (h % 4) * 128 + 128]
                        for j in range(2):
                            last = e.matmul(p_pv[:, j * 512:(j + 1) * 512], lhs,
                                            PT[kth][:, ktl * 1024 + j * 512:
                                                     ktl * 1024 + j * 512 + 512],
                                            start=(kth == 0 and ktl == 0),
                                            stop=(kth == 1 and ktl == 15))
                        return last
                    w = [exps_all[(h, qh, kth)][ktl], lV]
                    if kth == 0 and ktl == 0:
                        w.append(lastuse.get("p_pv"))
                    me = sch.ev("t", pm, waits=w)
                    lastuse[("PTrd", kth)] = me
                    lastuse[("vt4", sl4)] = me
                    lastuse[("pvlast", qh)] = me

                def dq_one(h, qh, qc, kth, i4):
                    exps = exps_all[(h, qh, kth)]

                    def dm(e, kth=kth, i4=i4, qc=qc):
                        last = None
                        for i in range(4):
                            ktl = i4 * 4 + i
                            last = e.matmul(p_d[0:1, 0:512], ones_qn[:],
                                            PT[kth][:, ktl * 1024 + qc * 512:
                                                     ktl * 1024 + qc * 512 + 512],
                                            start=(kth == 0 and ktl == 0),
                                            stop=(kth == 1 and ktl == 15))
                        return last
                    w = [exps[i4 * 4 + i] for i in range(4)]
                    if kth == 0 and i4 == 0:
                        w.append(lastuse.get(("pb", 0)))
                    me = sch.ev("t", dm, waits=w)
                    lastuse[("PTrd", kth)] = me
                    lastuse["dq_last"] = me

                def da_fin(h, qh, qc):
                    qi = qh * 2 + qc
                    da = sch.ev("v", (lambda e, qi=qi: e.tensor_copy(rrow[0:1, qi * 512:(qi + 1) * 512],
                                                                     p_d[0:1, 0:512])),
                                waits=[lastuse["dq_last"], lastuse.get(("rrow_rd", qi))])
                    lastuse[("pb", 0)] = da
                    lastuse[("da", qi)] = da

                def bc_fc(h, qh, qc):
                    qi = qh * 2 + qc
                    bc = sch.ev("t", (lambda e, qi=qi: e.matmul(p_f[:, 0:512], ones_row[:],
                                                                rrow[0:1, qi * 512:(qi + 1) * 512],
                                                                start=True, stop=True)),
                                waits=[lastuse[("da", qi)], lastuse.get(("pb", 1))])
                    lastuse[("rrow_rd", qi)] = bc
                    fc = sch.ev("v", (lambda e, qi=qi: e.reciprocal(fct[:, qi * 512:(qi + 1) * 512],
                                                                    p_f[:, 0:512])),
                                waits=[bc, lastuse.get(("fct_rd", qh))])
                    lastuse[("pb", 1)] = fc
                    lastuse[("fc", qi)] = fc

                def ae_dma(h, qh):
                    ae = sch.ev("v", (lambda e, qh=qh: e.tensor_tensor(
                        attn_sb[qh][:], p_pv[:], fct[:, qh * 1024:(qh + 1) * 1024], ALU.mult)),
                        waits=[lastuse[("pvlast", qh)], lastuse[("fc", qh * 2)],
                               lastuse[("fc", qh * 2 + 1)], lastuse.get(("attn_sb", qh))])
                    lastuse["p_pv"] = ae
                    lastuse[("fct_rd", qh)] = ae
                    yv = sch.ev("y", (lambda e, h=h, qh=qh: e.dma_start(
                        attnTs[h * 128:(h + 1) * 128, qh * 1024:(qh + 1) * 1024],
                        attn_sb[qh][:])), waits=[ae])
                    lastuse[("attn_sb", qh)] = yv

                def prevqh_fillers(ph_, pqh):
                    fl = [lambda: bc_fc(ph_, pqh, 0)]
                    fl += [lambda i=i: dq_one(ph_, pqh, 1, 1, i) for i in range(4)]
                    fl += [lambda: da_fin(ph_, pqh, 1), lambda: bc_fc(ph_, pqh, 1)]
                    fl += [lambda ktl=ktl: pv_one(ph_, pqh, 1, ktl) for ktl in range(16)]
                    fl += [lambda: ae_dma(ph_, pqh)]
                    return fl

                loads(0)
                loadv(0)
                b_sq(0)
                for j in range(12):
                    b_mm(0, j)
                b_fin(0)

                for h in range(H):
                    for qh in range(2):
                        for kth in range(2):
                            exps_all[(h, qh, kth)] = []
                            fill = []
                            if kth == 0:
                                if qh == 0:
                                    if h + 1 < H:
                                        loads(h + 1)
                                    if h > 0:
                                        fill += prevqh_fillers(h - 1, 1)
                                else:
                                    fill += prevqh_fillers(h, 0)
                                    if h + 1 < H:
                                        fill += bounds_fillers(h + 1)
                            tail = []
                            if kth == 1:
                                if qh == 0 and h % 4 == 0 and h + 4 < H:
                                    loadv(h // 4 + 1)
                                fill += [lambda i=i, h=h, qh=qh: dq_one(h, qh, 0, 0, i)
                                         for i in range(4)]
                                fill += [lambda ktl=ktl, h=h, qh=qh: pv_one(h, qh, 0, ktl)
                                         for ktl in range(16)]
                                fill += [lambda i=i, h=h, qh=qh: dq_one(h, qh, 0, 1, i)
                                         for i in range(2)]
                                tail += [lambda i=i, h=h, qh=qh: dq_one(h, qh, 0, 1, i)
                                         for i in range(2, 4)]
                                tail += [lambda h=h, qh=qh: da_fin(h, qh, 0)]
                                tail += [lambda i=i, h=h, qh=qh: dq_one(h, qh, 1, 0, i)
                                         for i in range(4)]
                            total = len(fill)
                            done = 0
                            for ktl in range(16):
                                qk_one(h, qh, kth, ktl)
                                want = (total * (ktl + 1) + 15) // 16
                                while done < want:
                                    fill[done]()
                                    done += 1
                            for f in tail:
                                f()
                for f in prevqh_fillers(H - 1, 1):
                    f()
                finish(sch)

        if MAX_PHASE >= 3:
            with ExitStack() as ph:
                A = ph.enter_context
                atn = A(nc.sbuf_tensor("atnC", [128, 32 * 1024], BF16))
                a_tok = A(nc.sbuf_tensor("a_tokC", [128, 8 * D], BF16))
                wslab = [A(nc.sbuf_tensor(f"wslabC{i}", [128, D], BF16)) for i in range(2)]
                aT = [A(nc.sbuf_tensor("aTC0", [128, 1024], BF16))] * 2
                xa = [A(nc.sbuf_tensor("xaC0", [128, D], F32))] * 2
                x2 = [A(nc.sbuf_tensor(f"x2C{i}", [128, D], F32)) for i in range(2)]
                bnb = [A(nc.sbuf_tensor(f"bnbC{i}", [128, 48], F32)) for i in range(2)]
                mv = [A(nc.sbuf_tensor(f"mvC{i}", [128, 2], F32)) for i in range(2)]
                st1 = [A(nc.sbuf_tensor(f"st1C{i}", [128, 4], F32)) for i in range(2)]
                p_trC = [A(nc.psum_tensor(f"p_trC{i}", [128, 1024], BF16)) for i in range(2)]

                a_tok_r = a_tok[:].rearrange("p (tt e) -> p tt e", tt=8)

                for hf in range(2):
                    sch = new_sched()
                    q0 = hf * 1024
                    lds = []
                    for et in range(DT):
                        ld = sch.ev("y", (lambda e, et=et, q0=q0:
                                          e.dma_start(atn[:, et * 1024:(et + 1) * 1024],
                                                      attnTs[et * 128:(et + 1) * 128, q0:q0 + 1024])),
                                    waits=[lastuse.get("atnC")])
                        lds.append(ld)
                    atok_done = []
                    for mt in range(DT):
                        sl = mt % 2
                        g1 = sch.ev("g", (lambda e, mt=mt, sl=sl:
                                          e.dma_start(wslab[sl][:], wo_p[mt, :, :])),
                                    waits=[lastuse.get(("wslabC", sl))])
                        pp = (p_s0, p_s1)[mt % 2]

                        def mm(e, pp=pp, sl=sl):
                            last = None
                            for kt in range(DT):
                                for j in range(2):
                                    last = e.matmul(pp[:, j * 512:(j + 1) * 512],
                                                    wslab[sl][:, kt * 128:(kt + 1) * 128],
                                                    atn[:, kt * 1024 + j * 512:kt * 1024 + j * 512 + 512],
                                                    start=(kt == 0), stop=(kt == DT - 1))
                            return last
                        me = sch.ev("t", mm, waits=[g1, lds[-1], lastuse.get(("ppC", id(pp)))])
                        lastuse[("wslabC", sl)] = me
                        se = sch.ev("s", (lambda e, pp=pp, sl=sl:
                                          e.activation(aT[sl][:], pp[:], AF.Copy)),
                                    waits=[me, lastuse.get(("aTC", 0)), lastuse.get(("aTC", 1))])
                        lastuse[("ppC", id(pp))] = se
                        bk = p_trC[mt % 2]

                        def tf(e, sl=sl, bk=bk):
                            last = None
                            for j in range(8):
                                last = e.transpose(bk[:, j * 128:(j + 1) * 128],
                                                   aT[sl][:, j * 128:(j + 1) * 128], eye_bf[:])
                            return last
                        tev = sch.ev("t", tf, waits=[se, lastuse.get(("p_trC", mt % 2)),
                                                     lastuse.get("a_tok")])
                        vev = sch.ev("v", (lambda e, mt=mt, bk=bk:
                                           e.tensor_copy(a_tok_r[:, 0:8, mt * 128:(mt + 1) * 128],
                                                         bk[:].rearrange("p (j q) -> p j q", j=8))),
                                     waits=[tev])
                        lastuse[("p_trC", mt % 2)] = vev
                        lastuse[("aTC", sl)] = vev
                        atok_done.append(vev)
                    lastuse["atnC"] = me

                    for tp in range(4):
                        v2s, s2s, v3s, v5s, s4s = [], [], [], [], []
                        row0a = q0 + tp * 256
                        ld_a = sch.ev("y", (lambda e, row0=row0a:
                                            e.dma_start(xa[0][:], x_all[row0:row0 + 128, :])),
                                      waits=[lastuse.get("xaC")])
                        for q in range(2):
                            tt = tp * 2 + q
                            v2s.append(ln_stats(
                                sch, (lambda j, tt=tt:
                                      a_tok[:, tt * D + j * 512:tt * D + (j + 1) * 512]),
                                bnb[q], mv[q], st1[q], [atok_done[-1]], pkey=("spC", q)))
                        for q in range(2):
                            tt = tp * 2 + q
                            s2 = sch.ev("s", (lambda e, tt=tt, q=q:
                                              e.activation(x2[q][:], a_tok[:, tt * D:(tt + 1) * D],
                                                           AF.Identity,
                                                           bias=st1[q][:, 3:4], scale=st1[q][:, 2:3])),
                                        waits=[v2s[q], lastuse.get(("x2C", q))])
                            lastuse[("spC", q)] = s2
                            s2s.append(s2)
                        v3a = sch.ev("v", (lambda e: e.tensor_add(x2[0][:], x2[0][:], xa[0][:])),
                                     waits=[s2s[0], ld_a])
                        v3s.append(v3a)
                        row0b = q0 + tp * 256 + 128
                        ld_b = sch.ev("y", (lambda e, row0=row0b:
                                            e.dma_start(xa[1][:], x_all[row0:row0 + 128, :])),
                                      waits=[v3a])
                        v3b = sch.ev("v", (lambda e: e.tensor_add(x2[1][:], x2[1][:], xa[1][:])),
                                     waits=[s2s[1], ld_b])
                        lastuse["xaC"] = v3b
                        v3s.append(v3b)
                        for q in range(2):
                            tt = tp * 2 + q
                            row0 = q0 + tt * 128
                            sch.ev("y", (lambda e, row0=row0, q=q:
                                         e.dma_start(x2s[row0:row0 + 128, :], x2[q][:])),
                                   waits=[v3s[q]])
                        for q in range(2):
                            v5s.append(ln_stats(
                                sch, (lambda j, q=q: x2[q][:, j * 512:(j + 1) * 512]),
                                bnb[q], mv[q], st1[q], [v3s[q]], pkey=("spC", q)))
                        for q in range(2):
                            tt = tp * 2 + q
                            s4 = sch.ev("s", (lambda e, q=q, tt=tt:
                                              e.activation(a_tok[:, tt * D:(tt + 1) * D], x2[q][:],
                                                           AF.Identity,
                                                           bias=st1[q][:, 3:4], scale=st1[q][:, 2:3])),
                                        waits=[v5s[q], lastuse.get(("ln2C", q))])
                            lastuse[("spC", q)] = s4
                            s4s.append(s4)
                        for q in range(2):
                            tt = tp * 2 + q
                            row0 = q0 + tt * 128
                            y2 = sch.ev("y", (lambda e, row0=row0, tt=tt:
                                              e.dma_start(ln2s[row0:row0 + 128, :],
                                                          a_tok[:, tt * D:(tt + 1) * D])),
                                        waits=[s4s[q]])
                            lastuse[("ln2C", q)] = y2
                            lastuse[("x2C", q)] = [s4s[q], y2]
                    lastuse["a_tok"] = y2
                    finish(sch)

        if MAX_PHASE >= 4:
            with ExitStack() as php:
                PD = php.enter_context
                p_trD = [PD(nc.psum_tensor(f"p_trD{i}", [128, 1024], BF16)) for i in range(2)]
                p_xD = PD(nc.psum_tensor("p_xD", [128, 512], F32))

                for ck in range(2):
                    c0 = ck * 1024
                    with ExitStack() as ph:
                        A = ph.enter_context
                        ln2T = A(nc.sbuf_tensor(f"ln2TD{ck}", [128, DT * 1024], BF16))
                        lt = [A(nc.sbuf_tensor(f"ltD{ck}_{i}", [128, D], BF16)) for i in range(2)]
                        xlatT = A(nc.sbuf_tensor(f"xlatTD{ck}", [128, LT * 1024], BF16))
                        eoT = A(nc.sbuf_tensor(f"eoTD{ck}", [128, LT * 1024], BF16))
                        slab = [A(nc.sbuf_tensor(f"slabD{ck}_{i}", [128, D], BF16)) for i in range(2)]
                        rsb = [A(nc.sbuf_tensor(f"rsbD{ck}_{i}", [128, 1024], BF16)) for i in range(2)]
                        hsb = [A(nc.sbuf_tensor(f"hsbD{ck}_{i}", [128, 1024], BF16)) for i in range(2)]
                        rtb = [A(nc.sbuf_tensor(f"rtbD{ck}_{i}", [128, 1024], F32)) for i in range(2)]
                        lg = A(nc.sbuf_tensor(f"lgD{ck}", [128, NE], F32))
                        tw = A(nc.sbuf_tensor(f"twD{ck}", [128, 24], F32))
                        rwc = A(nc.sbuf_tensor(f"rwcD{ck}", [128, 8], F32))
                        rwr = A(nc.sbuf_tensor(f"rwrD{ck}", [1, 1024], F32))
                        fct = A(nc.sbuf_tensor(f"fctD{ck}", [128, 1024], F32))

                        ln2T_r = ln2T[:].rearrange("p (dt t) -> p dt t", dt=DT)
                        sch = new_sched()
                        lu = lastuse

                        for tt8 in range(8):
                            sel = tt8 % 2
                            ld = sch.ev("y", (lambda e, tt8=tt8, sel=sel, c0=c0:
                                              e.dma_start(lt[sel][:],
                                                          ln2s[c0 + tt8 * 128:c0 + tt8 * 128 + 128, :])),
                                        waits=[lu.get(("ltD", sel))])
                            tevs = transpose_tile(
                                sch,
                                (lambda dt, sel=sel: lt[sel][:, dt * 128:(dt + 1) * 128]),
                                DT,
                                (lambda b8, tt8=tt8: ln2T_r[:, b8 * 8:(b8 + 1) * 8,
                                                            tt8 * 128:(tt8 + 1) * 128]),
                                [ld],
                                p_trD)
                            lu[("ltD", sel)] = tevs[-1]
                            lu[("ln2T_tile", tt8)] = tevs[-1]

                            def gmm(e, tt8=tt8):
                                last = None
                                for kt in range(DT):
                                    last = e.matmul(p_xD[:, 0:8],
                                                    ln2T[:, kt * 1024 + tt8 * 128:kt * 1024 + tt8 * 128 + 128],
                                                    gate_sb[:, kt * NE:(kt + 1) * NE],
                                                    start=(kt == 0), stop=(kt == DT - 1))
                                return last
                            ge = sch.ev("t", gmm, waits=[tevs[-1], lu.get("p_xD")])

                            prev = sch.ev("v", lambda e: e.tensor_copy(lg[:], p_xD[:, 0:8]),
                                          waits=[ge])
                            lu["p_xD"] = prev
                            prev = sch.ev("v", lambda e: e.tensor_tensor(
                                tw[:, 0:1], lg[:, 0:1], lg[:, 1:2], ALU.max), waits=[prev])
                            prev = sch.ev("v", lambda e: e.tensor_tensor(
                                tw[:, 8:9], lg[:, 0:1], lg[:, 1:2], ALU.min), waits=[prev])
                            for k in range(2, 8):
                                i = k - 2
                                prev = sch.ev("v", (lambda e, i=i, k=k: e.tensor_tensor(
                                    tw[:, 16:17], tw[:, i:i + 1], lg[:, k:k + 1], ALU.min)),
                                    waits=[prev])
                                prev = sch.ev("v", (lambda e, i=i: e.tensor_tensor(
                                    tw[:, 9 + i:10 + i], tw[:, 8 + i:9 + i], tw[:, 16:17], ALU.max)),
                                    waits=[prev])
                                prev = sch.ev("v", (lambda e, i=i, k=k: e.tensor_tensor(
                                    tw[:, i + 1:i + 2], tw[:, i:i + 1], lg[:, k:k + 1], ALU.max)),
                                    waits=[prev])
                            ve = sch.ev("v", (lambda e, tt8=tt8: e.tensor_tensor(
                                rwc[:, tt8:tt8 + 1], tw[:, 6:7], tw[:, 14:15], ALU.add)),
                                waits=[prev])
                            lu[("rw", tt8)] = ve
                        for tt8 in range(8):
                            ts = sch.ev("t", (lambda e, tt8=tt8:
                                              e.transpose(p_xD[0:1, (tt8 % 4) * 128:(tt8 % 4) * 128 + 128],
                                                          rwc[:, tt8:tt8 + 1], eye_f[:])),
                                        waits=[lu[("rw", tt8)], lu.get("p_xD")])
                            if tt8 % 4 == 3:
                                vr = sch.ev("v", (lambda e, tt8=tt8:
                                                  e.tensor_copy(rwr[0:1, (tt8 // 4) * 512:(tt8 // 4) * 512 + 512],
                                                                p_xD[0:1, 0:512])),
                                            waits=[ts])
                                lu["p_xD"] = vr
                        vf = None
                        for half in range(2):
                            tf2 = sch.ev("t", (lambda e, half=half:
                                               e.matmul(p_xD[:, 0:512], ones_row[:],
                                                        rwr[0:1, half * 512:half * 512 + 512],
                                                        start=True, stop=True)),
                                         waits=[vr, lu.get("p_xD")])
                            vf = sch.ev("v", (lambda e, half=half:
                                              e.tensor_copy(fct[:, half * 512:half * 512 + 512],
                                                            p_xD[:, 0:512])),
                                        waits=[tf2])
                            lu["p_xD"] = vf

                        ln2T_done = [lu[("ln2T_tile", t8)] for t8 in range(8)]

                        def gemmD(planes, MT, KT, rhs, rhs_evs, evac_fn, slab_w):
                            me = None
                            for mt in range(MT):
                                sl = mt % 2
                                g1 = sch.ev("g", (lambda e, planes=planes, mt=mt, sl=sl, slab_w=slab_w:
                                                  e.dma_start(slab[sl][:, 0:slab_w], planes[mt, :, :])),
                                            waits=[lu.get(("slabD", sl))])
                                pp = (p_s0, p_s1)[mt % 2]

                                def mm(e, pp=pp, sl=sl, KT=KT, rhs=rhs):
                                    last = None
                                    for kt in range(KT):
                                        lhs = slab[sl][:, kt * 128:(kt + 1) * 128]
                                        for j in range(2):
                                            last = e.matmul(pp[:, j * 512:(j + 1) * 512], lhs,
                                                            rhs(kt, j), start=(kt == 0), stop=(kt == KT - 1))
                                    return last
                                me = sch.ev("t", mm, waits=[g1, lu.get(("ppD", id(pp)))] + rhs_evs)
                                lu[("slabD", sl)] = me
                                free = evac_fn(mt, pp, me)
                                lu[("ppD", id(pp))] = free
                            return me

                        def fc1_evac(mt, pp, me):
                            ve = sch.ev("v", (lambda e, mt=mt, pp=pp:
                                              e.tensor_tensor(xlatT[:, mt * 1024:(mt + 1) * 1024],
                                                              pp[:], fct[:], ALU.mult)),
                                        waits=[me, vf, lu.get("xlatT")])
                            return ve
                        gemmD(fc1_p, LT, DT,
                              (lambda kt, j, c0=c0: ln2T[:, kt * 1024 + j * 512:kt * 1024 + j * 512 + 512]),
                              ln2T_done, fc1_evac, D)
                        xlat_done = [lu[("ppD", id(p_s0))], lu[("ppD", id(p_s1))]]

                        def exp_evac(mt, pp, me):
                            se = sch.ev("s", (lambda e, mt=mt, pp=pp:
                                              e.activation(eoT[:, mt * 1024:(mt + 1) * 1024], pp[:], AF.Copy)),
                                        waits=[me, lu.get("eoT")])
                            return se
                        exp_last = gemmD(exp_p, LT, LT,
                                         (lambda kt, j: xlatT[:, kt * 1024 + j * 512:kt * 1024 + j * 512 + 512]),
                                         xlat_done, exp_evac, LAT)
                        eo_done = [lu[("ppD", id(p_s0))], lu[("ppD", id(p_s1))]]
                        lu["xlatT"] = exp_last

                        def fc2_evac(mt, pp, me):
                            sl2 = mt % 2
                            se = sch.ev("s", (lambda e, pp=pp, sl2=sl2:
                                              e.activation(rtb[sl2][:], pp[:], AF.Copy)),
                                        waits=[me, lu.get(("rtbD", sl2))])
                            ye = sch.ev("y", (lambda e, mt=mt, sl2=sl2, c0=c0:
                                              e.dma_start(routedTs[mt * 128:(mt + 1) * 128, c0:c0 + 1024],
                                                          rtb[sl2][:])),
                                        waits=[se])
                            lu[("rtbD", sl2)] = ye
                            return se
                        fc2_last = gemmD(fc2_p, DT, LT,
                                         (lambda kt, j: eoT[:, kt * 1024 + j * 512:kt * 1024 + j * 512 + 512]),
                                         eo_done, fc2_evac, LAT)
                        lu["eoT"] = fc2_last

                        def up_evac(mt, pp, me):
                            sl2 = mt % 2
                            se = sch.ev("s", (lambda e, pp=pp, sl2=sl2:
                                              e.activation(rsb[sl2][:], pp[:], AF.Relu)),
                                        waits=[me, lu.get(("rsbD", sl2))])
                            ve = sch.ev("v", (lambda e, sl2=sl2:
                                              e.tensor_mul(hsb[sl2][:], rsb[sl2][:], rsb[sl2][:])),
                                        waits=[se, lu.get(("hsbD", sl2))])
                            lu[("rsbD", sl2)] = ve
                            ye = sch.ev("y", (lambda e, mt=mt, sl2=sl2, c0=c0:
                                              e.dma_start(hTs[mt * 128:(mt + 1) * 128, c0:c0 + 1024],
                                                          hsb[sl2][:])),
                                        waits=[ve])
                            lu[("hsbD", sl2)] = ye
                            lu[("hstore", mt)] = ye
                            return se
                        up_last = gemmD(up_p, IT, DT,
                                        (lambda kt, j: ln2T[:, kt * 1024 + j * 512:kt * 1024 + j * 512 + 512]),
                                        ln2T_done, up_evac, D)
                        finish(sch)

                    with ExitStack() as ph:
                        A = ph.enter_context
                        hT = A(nc.sbuf_tensor(f"hTD{ck}", [128, IT * 1024], BF16))
                        dslab = [A(nc.sbuf_tensor(f"dslabD{ck}_{i}", [128, INT], BF16)) for i in range(2)]
                        rt = [A(nc.sbuf_tensor(f"rtD{ck}_{i}", [128, 1024], F32)) for i in range(2)]
                        fpre = [A(nc.sbuf_tensor(f"fpreD{ck}_{i}", [128, 1024], BF16)) for i in range(2)]

                        sch = new_sched()
                        lu = lastuse
                        hlds = []
                        for et in range(IT):
                            ld = sch.ev("y", (lambda e, et=et, c0=c0:
                                              e.dma_start(hT[:, et * 1024:(et + 1) * 1024],
                                                          hTs[et * 128:(et + 1) * 128, c0:c0 + 1024])),
                                        waits=[lu.get(("hstore", et))])
                            hlds.append(ld)
                        for mt in range(DT):
                            sl = mt % 2
                            g1 = sch.ev("g", (lambda e, mt=mt, sl=sl:
                                              e.dma_start(dslab[sl][:], down_p[mt, :, :])),
                                        waits=[lu.get(("dslabD", sl))])
                            lr = sch.ev("y", (lambda e, mt=mt, sl=sl, c0=c0:
                                              e.dma_start(rt[sl][:],
                                                          routedTs[mt * 128:(mt + 1) * 128, c0:c0 + 1024])),
                                        waits=[lu.get(("rtD", sl))])
                            pp = (p_s0, p_s1)[mt % 2]

                            def mm_half(e, pp, sl, half):
                                last = None
                                for kt in range(half * IT // 2, (half + 1) * IT // 2):
                                    lhs = dslab[sl][:, kt * 128:(kt + 1) * 128]
                                    for j in range(2):
                                        last = e.matmul(pp[:, j * 512:(j + 1) * 512], lhs,
                                                        hT[:, kt * 1024 + j * 512:kt * 1024 + j * 512 + 512],
                                                        start=(kt == 0), stop=(kt == IT - 1))
                                return last
                            me = sch.ev("t", (lambda e, pp=pp, sl=sl: mm_half(e, pp, sl, 0)),
                                        waits=[g1, hlds[IT // 2 - 1], lu.get(("ppD2", id(pp)))])
                            me = sch.ev("t", (lambda e, pp=pp, sl=sl: mm_half(e, pp, sl, 1)),
                                        waits=[hlds[-1]])
                            lu[("dslabD", sl)] = me
                            ve = sch.ev("v", (lambda e, pp=pp, sl=sl:
                                              e.tensor_tensor(fpre[sl][:], pp[:], rt[sl][:], ALU.add)),
                                        waits=[me, lr, lu.get(("fpreD", sl))])
                            lu[("ppD2", id(pp))] = ve
                            lu[("rtD", sl)] = ve
                            ye = sch.ev("y", (lambda e, mt=mt, sl=sl, c0=c0:
                                              e.dma_start(fpreTs[mt * 128:(mt + 1) * 128, c0:c0 + 1024],
                                                          fpre[sl][:])),
                                        waits=[ve])
                            lu[("fpreD", sl)] = ye
                        finish(sch)

        if MAX_PHASE >= 5:
            with ExitStack() as ph:
                A = ph.enter_context
                fpT = A(nc.sbuf_tensor("fpTE", [128, DT * 512], BF16))
                f_tok = A(nc.sbuf_tensor("f_tokE", [128, 4 * D], BF16))
                gamma_b = A(nc.sbuf_tensor("gammaE", [128, D], F32))
                beta_b = A(nc.sbuf_tensor("betaE", [128, D], F32))
                x2a = [A(nc.sbuf_tensor("x2aE0", [128, D], F32))] * 2
                wv2 = [A(nc.sbuf_tensor(f"wv2E{i}", [128, D], F32)) for i in range(2)]
                osb = [A(nc.sbuf_tensor(f"osbE{i}", [128, D], F32)) for i in range(2)]
                bnb = [A(nc.sbuf_tensor(f"bnbE{i}", [128, 48], F32)) for i in range(2)]
                mv = [A(nc.sbuf_tensor(f"mvE{i}", [128, 2], F32)) for i in range(2)]
                st1 = [A(nc.sbuf_tensor(f"st1E{i}", [128, 4], F32)) for i in range(2)]
                p_trE = [A(nc.psum_tensor(f"p_trE{i}", [128, 1024], BF16)) for i in range(2)]

                f_tok_r = f_tok[:].rearrange("p (tt e) -> p tt e", tt=4)

                sch0 = new_sched()
                sch0.ev("y", lambda e: e.dma_start(gamma_b[:], gamma_h))
                sch0.ev("y", lambda e: e.dma_start(beta_b[:], beta_h))
                finish(sch0)

                out_stores = []
                sch = new_sched()
                for qr in range(4):
                    q0 = qr * 512
                    lds = []
                    for et in range(DT):
                        ld = sch.ev("y", (lambda e, et=et, q0=q0:
                                          e.dma_start(fpT[:, et * 512:(et + 1) * 512],
                                                      fpreTs[et * 128:(et + 1) * 128, q0:q0 + 512])),
                                    waits=[lastuse.get("fpTE")])
                        lds.append(ld)
                    ft_done = []
                    for tt in range(4):
                        def srcf(dt, tt=tt):
                            return fpT[:, dt * 512 + tt * 128:dt * 512 + tt * 128 + 128]
                        tevs = transpose_tile(
                            sch, srcf, DT,
                            (lambda b8, tt=tt: f_tok_r[:, tt:tt + 1, :].rearrange(
                                "p o (dt q) -> p (o dt) q", dt=DT)[:, b8 * 8:(b8 + 1) * 8, :]),
                            [lds[-1], lastuse.get(("f_tokE", tt))],
                            p_trE)
                        ft_done.append(tevs[-1])
                    lastuse["fpTE"] = tevs[-1]

                    for tp in range(2):
                        v2s, s2s, g3s, v5s, s4s = [], [], [], [], []
                        row0a = q0 + tp * 256
                        ldx_a = sch.ev("y", (lambda e, row0=row0a:
                                             e.dma_start(x2a[0][:], x2s[row0:row0 + 128, :])),
                                       waits=[lastuse.get("x2aE")])
                        for q in range(2):
                            tt = tp * 2 + q
                            v2s.append(ln_stats(
                                sch, (lambda j, tt=tt:
                                      f_tok[:, tt * D + j * 512:tt * D + (j + 1) * 512]),
                                bnb[q], mv[q], st1[q], [ft_done[tt]], pkey=("spE", q)))
                        for q in range(2):
                            tt = tp * 2 + q
                            s2 = sch.ev("s", (lambda e, tt=tt, q=q:
                                              e.activation(wv2[q][:], f_tok[:, tt * D:(tt + 1) * D],
                                                           AF.Identity,
                                                           bias=st1[q][:, 3:4], scale=st1[q][:, 2:3])),
                                        waits=[v2s[q], lastuse.get(("wvE", q))])
                            lastuse[("spE", q)] = s2
                            s2s.append(s2)
                        for q in range(2):
                            tt = tp * 2 + q
                            lastuse[("f_tokE", tt)] = s2s[q]
                            g3a = sch.ev("v", (lambda e, q=q:
                                               e.tensor_mul(wv2[q][:], wv2[q][:], gamma_b[:])),
                                         waits=[s2s[q]])
                            g3 = sch.ev("v", (lambda e, q=q:
                                              e.tensor_add(wv2[q][:], wv2[q][:], beta_b[:])),
                                        waits=[g3a])
                            g3s.append(g3)
                        for q in range(2):
                            v5s.append(ln_stats(
                                sch, (lambda j, q=q: wv2[q][:, j * 512:(j + 1) * 512]),
                                bnb[q], mv[q], st1[q], [g3s[q]], pkey=("spE", q)))
                        for q in range(2):
                            s4 = sch.ev("s", (lambda e, q=q:
                                              e.activation(osb[q][:], wv2[q][:], AF.Identity,
                                                           bias=st1[q][:, 3:4], scale=st1[q][:, 2:3])),
                                        waits=[v5s[q], lastuse.get(("osbE", q))])
                            lastuse[("wvE", q)] = s4
                            lastuse[("spE", q)] = s4
                            s4s.append(s4)
                        v6a = sch.ev("v", (lambda e: e.tensor_add(osb[0][:], osb[0][:], x2a[0][:])),
                                     waits=[s4s[0], ldx_a])
                        row0b = q0 + tp * 256 + 128
                        ldx_b = sch.ev("y", (lambda e, row0=row0b:
                                             e.dma_start(x2a[1][:], x2s[row0:row0 + 128, :])),
                                       waits=[v6a])
                        v6b = sch.ev("v", (lambda e: e.tensor_add(osb[1][:], osb[1][:], x2a[1][:])),
                                     waits=[s4s[1], ldx_b])
                        lastuse["x2aE"] = v6b
                        for q, v6 in ((0, v6a), (1, v6b)):
                            row0 = q0 + tp * 256 + q * 128
                            yo = sch.ev("y", (lambda e, row0=row0, q=q:
                                              e.dma_start(y_out[row0:row0 + 128, :], osb[q][:])),
                                        waits=[v6])
                            lastuse[("osbE", q)] = yo
                            out_stores.append(yo)
                finish(sch)

                schf = new_sched()
                schf.ev("y", lambda e: e.nop(),
                        waits=[("y", out_stores[-1].cnt)], inc=False)
                finish(schf)

    return nc


_CACHE = {}


def _prep(L0_attn_w_q, L0_attn_w_k, L0_attn_w_v, L0_attn_w_o, L0_ffn_gate_w,
          L0_ffn_up_w, L0_ffn_down_w, L0_ffn_fc1_w, L0_ffn_fc2_w, L0_ffn_expert_w,
          L0_ffn_ln_gamma, L0_ffn_ln_beta):
    f32 = lambda a: np.ascontiguousarray(np.asarray(a, dtype=np.float32))

    def attn_planes(w):
        w = f32(w)
        return np.ascontiguousarray(
            w.reshape(DT, 128, DT, 128).transpose(0, 3, 2, 1).reshape(DT, 128, D))

    def ffn_planes(w, KT, MT):
        w = f32(w)
        return np.ascontiguousarray(
            w.reshape(KT, 128, MT, 128).transpose(2, 1, 0, 3).reshape(MT, 128, KT * 128))

    d = {
        "wq_p": attn_planes(L0_attn_w_q),
        "wk_p": attn_planes(L0_attn_w_k),
        "wvf_p": np.ascontiguousarray(f32(L0_attn_w_v).T.reshape(DT, 128, D)),
        "wo_p": attn_planes(L0_attn_w_o),
        "up_p": ffn_planes(L0_ffn_up_w, DT, IT),
        "down_p": ffn_planes(L0_ffn_down_w, IT, DT),
        "fc1_p": ffn_planes(L0_ffn_fc1_w, DT, LT),
        "fc2_p": ffn_planes(L0_ffn_fc2_w, LT, DT),
        "exp_p": ffn_planes(L0_ffn_expert_w, LT, LT),
        "gate_h": np.ascontiguousarray(
            f32(L0_ffn_gate_w).reshape(DT, 128, NE).transpose(1, 0, 2).reshape(128, DT * NE)),
        "gamma_h": np.ascontiguousarray(np.broadcast_to(f32(L0_ffn_ln_gamma), (128, D))),
        "beta_h": np.ascontiguousarray(np.broadcast_to(f32(L0_ffn_ln_beta), (128, D))),
        "eye_h": np.eye(128, dtype=np.float32),
        "ones_col_h": np.ones((128, 1), np.float32),
        "eps_h": np.full((128, 1), EPS, np.float32),
        "ones_row_h": np.ones((1, 128), np.float32),
    }
    return d


def kernel(L0_attn_x, L0_attn_w_q, L0_attn_w_k, L0_attn_w_v, L0_attn_w_o,
           L0_ffn_gate_w, L0_ffn_up_w, L0_ffn_down_w, L0_ffn_fc1_w, L0_ffn_fc2_w,
           L0_ffn_expert_w, L0_ffn_ln_gamma, L0_ffn_ln_beta):
    x = np.ascontiguousarray(np.asarray(L0_attn_x, dtype=np.float32))
    shared = _prep(L0_attn_w_q, L0_attn_w_k, L0_attn_w_v, L0_attn_w_o, L0_ffn_gate_w,
                   L0_ffn_up_w, L0_ffn_down_w, L0_ffn_fc1_w, L0_ffn_fc2_w,
                   L0_ffn_expert_w, L0_ffn_ln_gamma, L0_ffn_ln_beta)
    nc = build_program()
    in_maps = []
    for c in range(8):
        b, hf = c // 2, c % 2
        xb = x[b]
        x_core = np.concatenate([xb[hf * T:(hf + 1) * T], xb[(1 - hf) * T:(2 - hf) * T]], axis=0)
        m = dict(shared)
        m["x_all"] = np.ascontiguousarray(x_core)
        in_maps.append(m)
    res = run_bass_kernel_spmd(nc, in_maps, core_ids=list(range(8)))
    out = np.empty((4, 4096, 4096), np.float32)
    for c in range(8):
        b, hf = c // 2, c % 2
        out[b, hf * T:(hf + 1) * T] = np.asarray(res.results[c]["y_out"])
    return out


# version 12
# speedup vs baseline: 1.0823x; 1.0823x over previous
import math
import numpy as np
from contextlib import ExitStack

import concourse.bass as bass
import concourse.mybir as mybir
from concourse.bass_utils import run_bass_kernel_spmd

F32 = mybir.dt.float32
BF16 = mybir.dt.bfloat16
AF = mybir.ActivationFunctionType
ALU = mybir.AluOpType
AX = mybir.AxisListType

D = 4096
S = 4096
T = 2048
H = 32
DK = 128
DT = D // 128
INT = 8192
IT = INT // 128
LAT = 2048
LT = LAT // 128
NE = 8
EPS = 1e-5
ISQ = 1.0 / math.sqrt(DK)
CB = math.sqrt(128.0)

ENG2BLOCK = {"t": "tensor", "s": "scalar", "v": "vector", "y": "sync", "g": "gpsimd"}
ENGINES = ("t", "s", "v", "y", "g")

DEBUG_TAPS = ()
MAX_PHASE = 99


class Ev:
    __slots__ = ("eng", "fn", "waits", "inc", "cnt")


class Sched:

    def __init__(self, nc, sems, cnt, entry):
        self.nc = nc
        self.sems = sems
        self.cnt = cnt
        self.entry = entry
        self.evs = []
        self._started = set()

    def ev(self, eng, fn, waits=(), inc=True):
        e = Ev()
        e.eng = eng
        e.fn = fn
        ws = []
        if eng not in self._started:
            self._started.add(eng)
            for k in ENGINES:
                if k != eng and self.entry.get(k, 0) > 0:
                    ws.append((k, self.entry[k]))

        def add(w):
            if w is None:
                return
            if isinstance(w, (list, tuple)) and not (len(w) == 2 and isinstance(w[0], str)):
                for w2 in w:
                    add(w2)
            elif isinstance(w, Ev):
                ws.append((w.eng, w.cnt))
            else:
                ws.append(w)
        for w in waits:
            add(w)
        e.waits = ws
        e.inc = inc
        if inc:
            self.cnt[eng] += 16 if eng in ("y", "g") else 1
        e.cnt = self.cnt[eng]
        self.evs.append(e)
        return e

    def emit(self, block):
        per = {k: [] for k in ENGINES}
        for e in self.evs:
            per[e.eng].append(e)
        for eng_key in ENGINES:
            evs = per[eng_key]
            if not evs:
                continue

            def body(engine, evs=evs, eng_key=eng_key, hw={}):
                for e in evs:
                    for (weng, wval) in e.waits:
                        if wval > hw.get(weng, 0):
                            engine.wait_ge(self.sems[weng], wval)
                            hw[weng] = wval
                    instr = e.fn(engine)
                    if e.inc:
                        instr.then_inc(self.sems[eng_key], 16 if eng_key in ("y", "g") else 1)

            getattr(block, ENG2BLOCK[eng_key])(body)


def build_program():
    nc = bass.Bass("TRN2", target_bir_lowering=False)

    def dram(name, shape, dtype, kind):
        if kind == "Internal" and name in DEBUG_TAPS:
            kind = "ExternalOutput"
        return nc.dram_tensor(name, shape, dtype, kind=kind).ap()

    x_all = dram("x_all", [S, D], F32, "ExternalInput")
    wq_p = dram("wq_p", [DT, 128, D], F32, "ExternalInput")
    wk_p = dram("wk_p", [DT, 128, D], F32, "ExternalInput")
    wvf_p = dram("wvf_p", [DT, 128, D], F32, "ExternalInput")
    wo_p = dram("wo_p", [DT, 128, D], F32, "ExternalInput")
    up_p = dram("up_p", [IT, 128, D], F32, "ExternalInput")
    down_p = dram("down_p", [DT, 128, INT], F32, "ExternalInput")
    fc1_p = dram("fc1_p", [LT, 128, D], F32, "ExternalInput")
    fc2_p = dram("fc2_p", [DT, 128, LAT], F32, "ExternalInput")
    exp_p = dram("exp_p", [LT, 128, LAT], F32, "ExternalInput")
    gate_h = dram("gate_h", [128, DT * NE], F32, "ExternalInput")
    gamma_h = dram("gamma_h", [128, D], F32, "ExternalInput")
    beta_h = dram("beta_h", [128, D], F32, "ExternalInput")
    eye_h = dram("eye_h", [128, 128], F32, "ExternalInput")
    ones_col_h = dram("ones_col_h", [128, 1], F32, "ExternalInput")
    eps_h = dram("eps_h", [128, 1], F32, "ExternalInput")
    ones_row_h = dram("ones_row_h", [1, 128], F32, "ExternalInput")

    y_out = dram("y_out", [T, D], F32, "ExternalOutput")

    qTs = dram("qTs", [D, T], BF16, "Internal")
    kTs = dram("kTs", [D, S], BF16, "Internal")
    vTs_tok = dram("vTs_tok", [S, D], BF16, "Internal")
    attnTs = dram("attnTs", [D, T], BF16, "Internal")
    x2s = dram("x2s", [T, D], F32, "Internal")
    ln2s = dram("ln2s", [T, D], BF16, "Internal")
    hTs = dram("hTs", [INT, T], BF16, "Internal")
    routedTs = dram("routedTs", [D, T], F32, "Internal")
    fpreTs = dram("fpreTs", [D, T], BF16, "Internal")

    sems = {}
    cnt = {k: 0 for k in ENGINES}

    with ExitStack() as top:
        E = top.enter_context
        for k in ENGINES:
            sems[k] = E(nc.semaphore("sem_" + k))

        eye_bf = E(nc.sbuf_tensor("eye_bf", [128, 128], BF16))
        eye_f = E(nc.sbuf_tensor("eye_f", [128, 128], F32))
        ones_qn = E(nc.sbuf_tensor("ones_qn", [128, 1], BF16))
        ones_row = E(nc.sbuf_tensor("ones_row", [1, 128], F32))
        eps_sb = E(nc.sbuf_tensor("eps_sb", [128, 1], F32))
        gate_sb = E(nc.sbuf_tensor("gate_sb", [128, DT * NE], BF16))

        p_s0 = E(nc.psum_tensor("p_s0", [128, 1024], F32))
        p_s1 = E(nc.psum_tensor("p_s1", [128, 1024], F32))

        block = E(nc.Block())
        nums = sorted(sems[k].num for k in ENGINES)
        assert nums[-1] - nums[0] == len(ENGINES) - 1
        nc.gpsimd.dma_reset(range(nums[0], nums[-1] + 1))
        nc.gpsimd.sem_clear(range(nums[0], nums[-1] + 1))
        nc.all_engine_barrier()

        snap = dict(cnt)

        def new_sched():
            return Sched(nc, sems, cnt, dict(snap))

        def finish(sch):
            nonlocal snap
            sch.emit(block)
            snap = dict(cnt)

        sch = new_sched()
        sch.ev("g", lambda e: e.dma_start(eye_bf[:], eye_h))
        sch.ev("g", lambda e: e.dma_start(ones_qn[:], ones_col_h))
        sch.ev("g", lambda e: e.dma_start(gate_sb[:], gate_h))
        sch.ev("y", lambda e: e.dma_start(eye_f[:], eye_h))
        sch.ev("y", lambda e: e.dma_start(ones_row[:], ones_row_h))
        sch.ev("y", lambda e: e.dma_start(eps_sb[:], eps_h))
        finish(sch)

        lastuse = {}

        def transpose_tile(sch, src_tile_fn, n_dt, dst_fn, waits_src, banks):
            evs = []
            for b8 in range(n_dt // 8):
                bk = banks[b8 % len(banks)]

                def tf(e, b8=b8, bk=bk):
                    last = None
                    for j in range(8):
                        last = e.transpose(bk[:, j * 128:(j + 1) * 128],
                                           src_tile_fn(b8 * 8 + j), eye_bf[:])
                    return last
                tev = sch.ev("t", tf, waits=list(waits_src) + [lastuse.get(("p_tr", id(bk)))])
                vev = sch.ev("v", (lambda e, b8=b8, bk=bk: e.tensor_copy(
                    dst_fn(b8), bk[:].rearrange("p (j q) -> p j q", j=8))),
                             waits=[tev])
                lastuse[("p_tr", id(bk))] = vev
                evs.append(vev)
            return evs

        def ln_stats(sch, src_fn, bnb, mv, st1, waits, pkey="statprev"):
            def stats(e):
                last = None
                for j in range(8):
                    last = e.bn_stats(bnb[:, j * 6:(j + 1) * 6], src_fn(j))
                return last
            v1a = sch.ev("v", stats, waits=list(waits))
            v1b = sch.ev("v", lambda e: e.bn_aggr(mv[:], bnb[:]), waits=[v1a])
            s1 = sch.ev("s", lambda e: e.activation(st1[:, 1:2], mv[:, 1:2], AF.Sqrt,
                                                    bias=eps_sb[:, 0:1]),
                        waits=[v1b])
            v2a = sch.ev("v", lambda e: e.reciprocal(st1[:, 2:3], st1[:, 1:2]),
                         waits=[s1, lastuse.get(pkey)])
            v2b = sch.ev("v", (lambda e: e.tensor_scalar(st1[:, 3:4], mv[:, 0:1], st1[:, 2:3],
                                                         -1.0, ALU.mult, ALU.mult)), waits=[v2a])
            return v2b

        if MAX_PHASE >= 1:
            with ExitStack() as ph:
                A = ph.enter_context
                ln1T = A(nc.sbuf_tensor("ln1T", [128, DT * 2048], BF16))
                p_trA = [A(nc.psum_tensor(f"p_trA{i}", [128, 1024], BF16)) for i in range(2)]

                ln1T_r = ln1T[:].rearrange("p (dt t) -> p dt t", dt=DT)

                for c in range(2):
                    with ExitStack() as lsc:
                        L = lsc.enter_context
                        xa = [L(nc.sbuf_tensor(f"xa{c}_{i}", [128, D], F32)) for i in range(2)]
                        ln1_sb = [L(nc.sbuf_tensor(f"ln1_{c}_{i}", [128, D], BF16)) for i in range(2)]
                        bnb = [L(nc.sbuf_tensor(f"bnbA{c}_{i}", [128, 48], F32)) for i in range(2)]
                        mv = [L(nc.sbuf_tensor(f"mvA{c}_{i}", [128, 2], F32)) for i in range(2)]
                        st1 = [L(nc.sbuf_tensor(f"st1A{c}_{i}", [128, 4], F32)) for i in range(2)]
                        wslab = [L(nc.sbuf_tensor(f"wslabA{c}_{i}", [128, D], BF16)) for i in range(2)]
                        evac = [L(nc.sbuf_tensor(f"evacA{c}_{i}", [128, 1024], BF16)) for i in range(2)]

                        sch = new_sched()
                        lnh = {0: [], 1: []}

                        def emit_ln(tp0, tp1):
                          for tp in range(tp0, tp1):
                            lds, v2s, s2s = [], [], []
                            for q in range(2):
                                tt = tp * 2 + q
                                row0 = c * 2048 + tt * 128
                                ld = sch.ev("y", (lambda e, row0=row0, q=q:
                                                  e.dma_start(xa[q][:], x_all[row0:row0 + 128, :])),
                                            waits=[lastuse.get(("xa", q))])
                                lds.append(ld)
                            for q in range(2):
                                v2s.append(ln_stats(
                                    sch, (lambda j, q=q: xa[q][:, j * 512:(j + 1) * 512]),
                                    bnb[q], mv[q], st1[q], [lds[q]], pkey=("spA", q)))
                            for q in range(2):
                                s2 = sch.ev("s", (lambda e, q=q:
                                                  e.activation(ln1_sb[q][:], xa[q][:], AF.Identity,
                                                               bias=st1[q][:, 3:4],
                                                               scale=st1[q][:, 2:3])),
                                            waits=[v2s[q], lastuse.get(("ln1_sb", q))])
                                lastuse[("spA", q)] = s2
                                lastuse[("xa", q)] = s2
                                s2s.append(s2)
                            for q in range(2):
                                tt = tp * 2 + q
                                tevs = transpose_tile(
                                    sch,
                                    (lambda dt, q=q: ln1_sb[q][:, dt * 128:(dt + 1) * 128]),
                                    DT,
                                    (lambda b8, tt=tt: ln1T_r[:, b8 * 8:(b8 + 1) * 8,
                                                              tt * 128:(tt + 1) * 128]),
                                    [s2s[q]],
                                    p_trA)
                                lastuse[("ln1_sb", q)] = tevs[-1]
                                lnh[tt // 8].append(tevs[-1])

                        def emit_pass_th(planes, spill, scale, colbase, th):
                            for mt in range(DT):
                                sl = mt % 2
                                g1 = sch.ev("g", (lambda e, planes=planes, mt=mt, sl=sl:
                                                  e.dma_start(wslab[sl][:], planes[mt, :, :])),
                                            waits=[lastuse.get(("wslabA", sl))])
                                pp = (p_s0, p_s1)[mt % 2]

                                def mm(e, pp=pp, sl=sl, th=th):
                                    last = None
                                    for kt in range(DT):
                                        lhs = wslab[sl][:, kt * 128:(kt + 1) * 128]
                                        for j in range(2):
                                            rhs = ln1T[:, kt * 2048 + th * 1024 + j * 512:
                                                       kt * 2048 + th * 1024 + j * 512 + 512]
                                            last = e.matmul(pp[:, j * 512:(j + 1) * 512], lhs, rhs,
                                                            start=(kt == 0), stop=(kt == DT - 1))
                                    return last
                                me = sch.ev("t", mm, waits=[g1, lastuse.get(("ppA", id(pp)))] +
                                            lnh[th])
                                lastuse[("wslabA", sl)] = me
                                esl = mt % 2
                                se = sch.ev("s", (lambda e, pp=pp, esl=esl, scale=scale:
                                                  e.activation(evac[esl][:], pp[:], AF.Copy, scale=scale)),
                                            waits=[me, lastuse.get(("evacA", esl))])
                                lastuse[("ppA", id(pp))] = se
                                ye = sch.ev("y", (lambda e, spill=spill, mt=mt, th=th, esl=esl, colbase=colbase:
                                                  e.dma_start(spill[mt * 128:(mt + 1) * 128,
                                                              colbase + th * 1024: colbase + th * 1024 + 1024],
                                                              evac[esl][:])),
                                            waits=[se])
                                lastuse[("evacA", esl)] = ye

                        passes = []
                        if c == 0:
                            passes.append((wq_p, qTs, ISQ, 0))
                        passes.append((wk_p, kTs, 1.0, c * 2048))

                        emit_ln(0, 4)
                        first = True
                        for (planes, spill, scale, colbase) in passes:
                            emit_pass_th(planes, spill, scale, colbase, 0)
                            if first:
                                emit_ln(4, 8)
                                first = False
                            emit_pass_th(planes, spill, scale, colbase, 1)
                        finish(sch)

                    with ExitStack() as vsc:
                        V = vsc.enter_context
                        wvslab = [V(nc.sbuf_tensor(f"wvslab{c}_{i}", [128, DT * 512], BF16))
                                  for i in range(2)]
                        vsb = [V(nc.sbuf_tensor(f"vsb{c}_{i}", [128, 512], BF16)) for i in range(2)]

                        sch = new_sched()
                        for dc in range(8):
                            sl = dc % 2
                            g1 = sch.ev("g", (lambda e, dc=dc, sl=sl:
                                              e.dma_start(
                                                  wvslab[sl][:].rearrange("p (kt q) -> p kt q", kt=DT),
                                                  wvf_p[:, :, dc * 512:(dc + 1) * 512]
                                                  .rearrange("kt p q -> p kt q"))),
                                        waits=[lastuse.get(("wvslab", sl))])
                            last_mm = None
                            for tt in range(16):
                                slot = (dc * 16 + tt) % 4
                                pp = (p_s0, p_s1)[slot // 2]
                                col = (slot % 2) * 512

                                def mm(e, pp=pp, col=col, sl=sl, tt=tt):
                                    last = None
                                    for kt in range(DT):
                                        last = e.matmul(pp[:, col:col + 512],
                                                        ln1T[:, kt * 2048 + tt * 128:kt * 2048 + tt * 128 + 128],
                                                        wvslab[sl][:, kt * 512:(kt + 1) * 512],
                                                        start=(kt == 0), stop=(kt == DT - 1))
                                    return last
                                me = sch.ev("t", mm, waits=[g1, lastuse.get(("slotV", slot))])
                                last_mm = me
                                esl = (dc * 16 + tt) % 2
                                eng = "s" if (tt % 2 == 0) else "v"
                                if eng == "s":
                                    ee = sch.ev("s", (lambda e, pp=pp, col=col, esl=esl:
                                                      e.activation(vsb[esl][:], pp[:, col:col + 512], AF.Copy)),
                                                waits=[me, lastuse.get(("vsb", esl))])
                                else:
                                    ee = sch.ev("v", (lambda e, pp=pp, col=col, esl=esl:
                                                      e.tensor_copy(vsb[esl][:], pp[:, col:col + 512])),
                                                waits=[me, lastuse.get(("vsb", esl))])
                                lastuse[("slotV", slot)] = ee
                                ye = sch.ev("y", (lambda e, c=c, tt=tt, dc=dc, esl=esl:
                                                  e.dma_start(vTs_tok[c * 2048 + tt * 128:c * 2048 + tt * 128 + 128,
                                                              dc * 512:(dc + 1) * 512],
                                                              vsb[esl][:])),
                                            waits=[ee])
                                lastuse[("vsb", esl)] = ye
                            lastuse[("wvslab", sl)] = last_mm
                        finish(sch)

        if MAX_PHASE >= 2:
            with ExitStack() as ph:
                A = ph.enter_context
                kT = [A(nc.sbuf_tensor(f"kT{i}", [128, S], BF16)) for i in range(2)]
                qT = [A(nc.sbuf_tensor(f"qT{i}", [128, T], BF16)) for i in range(2)]
                vt4 = [A(nc.sbuf_tensor(f"vt4_{i}", [128, 32 * 512], BF16)) for i in range(2)]
                PT = [A(nc.sbuf_tensor(f"PT{i}", [128, 16 * 1024], BF16)) for i in range(2)]
                sqk = A(nc.sbuf_tensor("sqk", [128, S], BF16))
                sqq = A(nc.sbuf_tensor("sqq", [128, T], BF16))
                kn2 = A(nc.sbuf_tensor("kn2", [1, S], BF16))
                qn2 = A(nc.sbuf_tensor("qn2", [1, T], BF16))
                kst = A(nc.sbuf_tensor("kst", [1, 8], F32))
                nbh = A(nc.sbuf_tensor("nbh", [128, 2], F32))
                rrow = A(nc.sbuf_tensor("rrow", [1, 2048], F32))
                fct = A(nc.sbuf_tensor("fct", [128, 2048], F32))
                attn_sb = [A(nc.sbuf_tensor(f"attn{i}", [128, 1024], BF16)) for i in range(2)]

                p_pv = A(nc.psum_tensor("p_pv", [128, 1024], F32))
                p_d = A(nc.psum_tensor("p_d", [128, 512], F32))
                p_f = A(nc.psum_tensor("p_f", [128, 512], F32))

                sch = new_sched()
                psi = [0]
                exps_all = {}

                def loads(h):
                    sl = h % 2
                    lK = sch.ev("y", (lambda e, h=h, sl=sl:
                                      e.dma_start(kT[sl][:], kTs[h * 128:(h + 1) * 128, :])),
                                waits=[lastuse.get(("kT", sl))])
                    lQ = sch.ev("y", (lambda e, h=h, sl=sl:
                                      e.dma_start(qT[sl][:], qTs[h * 128:(h + 1) * 128, :])),
                                waits=[lastuse.get(("qT", sl))])
                    lastuse[("ld", h)] = (lK, lQ)

                def loadv(hg):
                    sl = hg % 2
                    lV = sch.ev("g", (lambda e, hg=hg, sl=sl:
                                      e.dma_start(
                                          vt4[sl][:].rearrange("p (kt q) -> p kt q", kt=32),
                                          vTs_tok.rearrange("(kt p) d -> p kt d", p=128)
                                          [:, :, hg * 512:(hg + 1) * 512])),
                                waits=[lastuse.get(("vt4", sl))])
                    lastuse[("lv", hg)] = lV

                def b_sq(h):
                    sl = h % 2
                    lK, lQ = lastuse[("ld", h)]
                    sq1 = sch.ev("v", (lambda e, sl=sl: e.tensor_mul(sqk[:], kT[sl][:], kT[sl][:])),
                                 waits=[lK, lastuse.get("sqk_rd")])
                    sq2 = sch.ev("v", (lambda e, sl=sl: e.tensor_mul(sqq[:], qT[sl][:], qT[sl][:])),
                                 waits=[lQ, lastuse.get("sqq_rd")])
                    lastuse[("bsq", h)] = (sq1, sq2)

                def b_mm(h, j):
                    sq1, sq2 = lastuse[("bsq", h)]
                    if j < 8:
                        src, dst, sqe = sqk, kn2, sq1
                        jj = j
                    else:
                        src, dst, sqe = sqq, qn2, sq2
                        jj = j - 8
                    me = sch.ev("t", (lambda e, src=src, jj=jj:
                                      e.matmul(p_f[0:1, 0:512], ones_qn[:],
                                               src[:, jj * 512:(jj + 1) * 512], start=True, stop=True)),
                                waits=[sqe, lastuse.get(("pb", 1))])
                    ce = sch.ev("v", (lambda e, dst=dst, jj=jj:
                                      e.tensor_copy(dst[0:1, jj * 512:(jj + 1) * 512], p_f[0:1, 0:512])),
                                waits=[me])
                    lastuse[("pb", 1)] = ce
                    lastuse[("bcp", h, j)] = ce
                    if j == 7:
                        lastuse["sqk_rd"] = me
                    if j == 11:
                        lastuse["sqq_rd"] = me

                def b_fin(h):
                    kevs = [lastuse[("bcp", h, j)] for j in range(8)]
                    qevs = [lastuse[("bcp", h, j)] for j in range(8, 12)]
                    v1 = sch.ev("v", lambda e: e.reduce_max(kst[0:1, 0:1], kn2[0:1, :], axis=AX.X),
                                waits=kevs + [lastuse.get("kst_rd")])
                    v2 = sch.ev("v", lambda e: e.reduce_max(kst[0:1, 1:2], qn2[0:1, :], axis=AX.X),
                                waits=qevs)
                    v3 = sch.ev("v", (lambda e: e.tensor_scalar(kst[0:1, 2:3], kst[0:1, 0:1],
                                                                -0.5 / CB, None, ALU.mult)),
                                waits=[v1])
                    v4 = sch.ev("v", (lambda e: e.tensor_scalar(kst[0:1, 3:4], kst[0:1, 1:2],
                                                                -0.5 * CB, kst[0:1, 2:3],
                                                                ALU.mult, ALU.add)),
                                waits=[v2, v3])
                    tb = sch.ev("t", (lambda e: e.matmul(p_f[:, 0:1], ones_row[:], kst[0:1, 3:4],
                                                         start=True, stop=True)),
                                waits=[v4, lastuse.get(("pb", 1))])
                    lastuse["kst_rd"] = tb
                    hp = h % 2
                    nb = sch.ev("v", (lambda e, hp=hp: e.tensor_copy(nbh[:, hp:hp + 1], p_f[:, 0:1])),
                                waits=[tb, lastuse.get(("nbh_rd", hp))])
                    lastuse[("pb", 1)] = nb
                    lastuse[("nbh", h)] = nb

                def bounds_fillers(h):
                    return ([lambda h=h: b_sq(h)] +
                            [lambda h=h, j=j: b_mm(h, j) for j in range(12)] +
                            [lambda h=h: b_fin(h)])

                def qk_one(h, qh, kth, ktl):
                    sl = h % 2
                    lK, lQ = lastuse[("ld", h)]
                    nbev = lastuse[("nbh", h)]
                    hp = h % 2
                    kt = kth * 16 + ktl
                    pidx = psi[0] % 2
                    psi[0] += 1
                    pp = (p_s0, p_s1)[pidx]

                    def mmf(e, pp=pp, sl=sl, kt=kt, qh=qh):
                        last = None
                        for qc in range(2):
                            last = e.matmul(pp[:, qc * 512:(qc + 1) * 512],
                                            kT[sl][:, kt * 128:(kt + 1) * 128],
                                            qT[sl][:, qh * 1024 + qc * 512:qh * 1024 + qc * 512 + 512],
                                            start=True, stop=True)
                        return last
                    me = sch.ev("t", mmf, waits=[lK, lQ, lastuse.get(("psB", pidx))])
                    se = sch.ev("s", (lambda e, pp=pp, kth=kth, ktl=ktl, hp=hp:
                                      e.activation(PT[kth][:, ktl * 1024:(ktl + 1) * 1024],
                                                   pp[:], AF.Exp, bias=nbh[:, hp:hp + 1])),
                                waits=[me, lastuse.get(("PTrd", kth)), nbev])
                    lastuse[("psB", pidx)] = se
                    lastuse[("kT", sl)] = me
                    lastuse[("qT", sl)] = me
                    lastuse[("nbh_rd", hp)] = se
                    exps_all[(h, qh, kth)].append(se)

                def pv_one(h, qh, kth, ktl):
                    sl4 = (h // 4) % 2
                    lV = lastuse[("lv", h // 4)]

                    def pm(e, kth=kth, ktl=ktl, sl4=sl4, h=h):
                        last = None
                        lhs = vt4[sl4][:, (kth * 16 + ktl) * 512 + (h % 4) * 128:
                                       (kth * 16 + ktl) * 512 + (h % 4) * 128 + 128]
                        for j in range(2):
                            last = e.matmul(p_pv[:, j * 512:(j + 1) * 512], lhs,
                                            PT[kth][:, ktl * 1024 + j * 512:
                                                     ktl * 1024 + j * 512 + 512],
                                            start=(kth == 0 and ktl == 0),
                                            stop=(kth == 1 and ktl == 15))
                        return last
                    w = [exps_all[(h, qh, kth)][ktl], lV]
                    if kth == 0 and ktl == 0:
                        w.append(lastuse.get("p_pv"))
                    me = sch.ev("t", pm, waits=w)
                    lastuse[("PTrd", kth)] = me
                    lastuse[("vt4", sl4)] = me
                    lastuse[("pvlast", qh)] = me

                def dq_one(h, qh, qc, kth, i4):
                    exps = exps_all[(h, qh, kth)]

                    def dm(e, kth=kth, i4=i4, qc=qc):
                        last = None
                        for i in range(4):
                            ktl = i4 * 4 + i
                            last = e.matmul(p_d[0:1, 0:512], ones_qn[:],
                                            PT[kth][:, ktl * 1024 + qc * 512:
                                                     ktl * 1024 + qc * 512 + 512],
                                            start=(kth == 0 and ktl == 0),
                                            stop=(kth == 1 and ktl == 15))
                        return last
                    w = [exps[i4 * 4 + i] for i in range(4)]
                    if kth == 0 and i4 == 0:
                        w.append(lastuse.get(("pb", 0)))
                    me = sch.ev("t", dm, waits=w)
                    lastuse[("PTrd", kth)] = me
                    lastuse["dq_last"] = me

                def da_fin(h, qh, qc):
                    qi = qh * 2 + qc
                    da = sch.ev("v", (lambda e, qi=qi: e.tensor_copy(rrow[0:1, qi * 512:(qi + 1) * 512],
                                                                     p_d[0:1, 0:512])),
                                waits=[lastuse["dq_last"], lastuse.get(("rrow_rd", qi))])
                    lastuse[("pb", 0)] = da
                    lastuse[("da", qi)] = da

                def bc_fc(h, qh, qc):
                    qi = qh * 2 + qc
                    bc = sch.ev("t", (lambda e, qi=qi: e.matmul(p_f[:, 0:512], ones_row[:],
                                                                rrow[0:1, qi * 512:(qi + 1) * 512],
                                                                start=True, stop=True)),
                                waits=[lastuse[("da", qi)], lastuse.get(("pb", 1))])
                    lastuse[("rrow_rd", qi)] = bc
                    fc = sch.ev("v", (lambda e, qi=qi: e.reciprocal(fct[:, qi * 512:(qi + 1) * 512],
                                                                    p_f[:, 0:512])),
                                waits=[bc, lastuse.get(("fct_rd", qh))])
                    lastuse[("pb", 1)] = fc
                    lastuse[("fc", qi)] = fc

                def ae_dma(h, qh):
                    ae = sch.ev("v", (lambda e, qh=qh: e.tensor_tensor(
                        attn_sb[qh][:], p_pv[:], fct[:, qh * 1024:(qh + 1) * 1024], ALU.mult)),
                        waits=[lastuse[("pvlast", qh)], lastuse[("fc", qh * 2)],
                               lastuse[("fc", qh * 2 + 1)], lastuse.get(("attn_sb", qh))])
                    lastuse["p_pv"] = ae
                    lastuse[("fct_rd", qh)] = ae
                    yv = sch.ev("y", (lambda e, h=h, qh=qh: e.dma_start(
                        attnTs[h * 128:(h + 1) * 128, qh * 1024:(qh + 1) * 1024],
                        attn_sb[qh][:])), waits=[ae])
                    lastuse[("attn_sb", qh)] = yv

                def prevqh_fillers(ph_, pqh):
                    fl = [lambda: bc_fc(ph_, pqh, 0)]
                    fl += [lambda i=i: dq_one(ph_, pqh, 1, 1, i) for i in range(4)]
                    fl += [lambda: da_fin(ph_, pqh, 1), lambda: bc_fc(ph_, pqh, 1)]
                    fl += [lambda ktl=ktl: pv_one(ph_, pqh, 1, ktl) for ktl in range(16)]
                    fl += [lambda: ae_dma(ph_, pqh)]
                    return fl

                loads(0)
                loadv(0)
                b_sq(0)
                for j in range(12):
                    b_mm(0, j)
                b_fin(0)

                for h in range(H):
                    for qh in range(2):
                        for kth in range(2):
                            exps_all[(h, qh, kth)] = []
                            fill = []
                            if kth == 0:
                                if qh == 0:
                                    if h + 1 < H:
                                        loads(h + 1)
                                    if h > 0:
                                        fill += prevqh_fillers(h - 1, 1)
                                else:
                                    fill += prevqh_fillers(h, 0)
                                    if h + 1 < H:
                                        fill += bounds_fillers(h + 1)
                            tail = []
                            if kth == 1:
                                if qh == 0 and h % 4 == 0 and h + 4 < H:
                                    loadv(h // 4 + 1)
                                fill += [lambda i=i, h=h, qh=qh: dq_one(h, qh, 0, 0, i)
                                         for i in range(4)]
                                fill += [lambda ktl=ktl, h=h, qh=qh: pv_one(h, qh, 0, ktl)
                                         for ktl in range(16)]
                                fill += [lambda i=i, h=h, qh=qh: dq_one(h, qh, 0, 1, i)
                                         for i in range(2)]
                                tail += [lambda i=i, h=h, qh=qh: dq_one(h, qh, 0, 1, i)
                                         for i in range(2, 4)]
                                tail += [lambda h=h, qh=qh: da_fin(h, qh, 0)]
                                tail += [lambda i=i, h=h, qh=qh: dq_one(h, qh, 1, 0, i)
                                         for i in range(4)]
                            total = len(fill)
                            done = 0
                            for ktl in range(16):
                                qk_one(h, qh, kth, ktl)
                                want = (total * (ktl + 1) + 15) // 16
                                while done < want:
                                    fill[done]()
                                    done += 1
                            for f in tail:
                                f()
                for f in prevqh_fillers(H - 1, 1):
                    f()
                finish(sch)

        if MAX_PHASE >= 3:
            with ExitStack() as ph:
                A = ph.enter_context
                atn = A(nc.sbuf_tensor("atnC", [128, 32 * 1024], BF16))
                a_tok = A(nc.sbuf_tensor("a_tokC", [128, 8 * D], BF16))
                wslab = [A(nc.sbuf_tensor(f"wslabC{i}", [128, D], BF16)) for i in range(2)]
                aT = [A(nc.sbuf_tensor("aTC0", [128, 1024], BF16))] * 2
                xa = [A(nc.sbuf_tensor("xaC0", [128, D], F32))] * 2
                x2 = [A(nc.sbuf_tensor(f"x2C{i}", [128, D], F32)) for i in range(2)]
                bnb = [A(nc.sbuf_tensor(f"bnbC{i}", [128, 48], F32)) for i in range(2)]
                mv = [A(nc.sbuf_tensor(f"mvC{i}", [128, 2], F32)) for i in range(2)]
                st1 = [A(nc.sbuf_tensor(f"st1C{i}", [128, 4], F32)) for i in range(2)]
                p_trC = [A(nc.psum_tensor(f"p_trC{i}", [128, 1024], BF16)) for i in range(2)]

                a_tok_r = a_tok[:].rearrange("p (tt e) -> p tt e", tt=8)

                for hf in range(2):
                    sch = new_sched()
                    q0 = hf * 1024
                    lds = []
                    for et in range(DT):
                        ld = sch.ev("y", (lambda e, et=et, q0=q0:
                                          e.dma_start(atn[:, et * 1024:(et + 1) * 1024],
                                                      attnTs[et * 128:(et + 1) * 128, q0:q0 + 1024])),
                                    waits=[lastuse.get("atnC")])
                        lds.append(ld)
                    atok_done = []
                    for mt in range(DT):
                        sl = mt % 2
                        g1 = sch.ev("g", (lambda e, mt=mt, sl=sl:
                                          e.dma_start(wslab[sl][:], wo_p[mt, :, :])),
                                    waits=[lastuse.get(("wslabC", sl))])
                        pp = (p_s0, p_s1)[mt % 2]

                        def mm(e, pp=pp, sl=sl):
                            last = None
                            for kt in range(DT):
                                for j in range(2):
                                    last = e.matmul(pp[:, j * 512:(j + 1) * 512],
                                                    wslab[sl][:, kt * 128:(kt + 1) * 128],
                                                    atn[:, kt * 1024 + j * 512:kt * 1024 + j * 512 + 512],
                                                    start=(kt == 0), stop=(kt == DT - 1))
                            return last
                        me = sch.ev("t", mm, waits=[g1, lds[-1], lastuse.get(("ppC", id(pp)))])
                        lastuse[("wslabC", sl)] = me
                        se = sch.ev("s", (lambda e, pp=pp, sl=sl:
                                          e.activation(aT[sl][:], pp[:], AF.Copy)),
                                    waits=[me, lastuse.get(("aTC", 0)), lastuse.get(("aTC", 1))])
                        lastuse[("ppC", id(pp))] = se
                        bk = p_trC[mt % 2]

                        def tf(e, sl=sl, bk=bk):
                            last = None
                            for j in range(8):
                                last = e.transpose(bk[:, j * 128:(j + 1) * 128],
                                                   aT[sl][:, j * 128:(j + 1) * 128], eye_bf[:])
                            return last
                        tev = sch.ev("t", tf, waits=[se, lastuse.get(("p_trC", mt % 2)),
                                                     lastuse.get("a_tok")])
                        vev = sch.ev("v", (lambda e, mt=mt, bk=bk:
                                           e.tensor_copy(a_tok_r[:, 0:8, mt * 128:(mt + 1) * 128],
                                                         bk[:].rearrange("p (j q) -> p j q", j=8))),
                                     waits=[tev])
                        lastuse[("p_trC", mt % 2)] = vev
                        lastuse[("aTC", sl)] = vev
                        atok_done.append(vev)
                    lastuse["atnC"] = me

                    for tp in range(4):
                        v2s, s2s, v3s, v5s, s4s = [], [], [], [], []
                        row0a = q0 + tp * 256
                        ld_a = sch.ev("y", (lambda e, row0=row0a:
                                            e.dma_start(xa[0][:], x_all[row0:row0 + 128, :])),
                                      waits=[lastuse.get("xaC")])
                        for q in range(2):
                            tt = tp * 2 + q
                            v2s.append(ln_stats(
                                sch, (lambda j, tt=tt:
                                      a_tok[:, tt * D + j * 512:tt * D + (j + 1) * 512]),
                                bnb[q], mv[q], st1[q], [atok_done[-1]], pkey=("spC", q)))
                        for q in range(2):
                            tt = tp * 2 + q
                            s2 = sch.ev("s", (lambda e, tt=tt, q=q:
                                              e.activation(x2[q][:], a_tok[:, tt * D:(tt + 1) * D],
                                                           AF.Identity,
                                                           bias=st1[q][:, 3:4], scale=st1[q][:, 2:3])),
                                        waits=[v2s[q], lastuse.get(("x2C", q))])
                            lastuse[("spC", q)] = s2
                            s2s.append(s2)
                        v3a = sch.ev("v", (lambda e: e.tensor_add(x2[0][:], x2[0][:], xa[0][:])),
                                     waits=[s2s[0], ld_a])
                        v3s.append(v3a)
                        row0b = q0 + tp * 256 + 128
                        ld_b = sch.ev("y", (lambda e, row0=row0b:
                                            e.dma_start(xa[1][:], x_all[row0:row0 + 128, :])),
                                      waits=[v3a])
                        v3b = sch.ev("v", (lambda e: e.tensor_add(x2[1][:], x2[1][:], xa[1][:])),
                                     waits=[s2s[1], ld_b])
                        lastuse["xaC"] = v3b
                        v3s.append(v3b)
                        for q in range(2):
                            tt = tp * 2 + q
                            row0 = q0 + tt * 128
                            sch.ev("y", (lambda e, row0=row0, q=q:
                                         e.dma_start(x2s[row0:row0 + 128, :], x2[q][:])),
                                   waits=[v3s[q]])
                        for q in range(2):
                            v5s.append(ln_stats(
                                sch, (lambda j, q=q: x2[q][:, j * 512:(j + 1) * 512]),
                                bnb[q], mv[q], st1[q], [v3s[q]], pkey=("spC", q)))
                        for q in range(2):
                            tt = tp * 2 + q
                            s4 = sch.ev("s", (lambda e, q=q, tt=tt:
                                              e.activation(a_tok[:, tt * D:(tt + 1) * D], x2[q][:],
                                                           AF.Identity,
                                                           bias=st1[q][:, 3:4], scale=st1[q][:, 2:3])),
                                        waits=[v5s[q], lastuse.get(("ln2C", q))])
                            lastuse[("spC", q)] = s4
                            s4s.append(s4)
                        for q in range(2):
                            tt = tp * 2 + q
                            row0 = q0 + tt * 128
                            y2 = sch.ev("y", (lambda e, row0=row0, tt=tt:
                                              e.dma_start(ln2s[row0:row0 + 128, :],
                                                          a_tok[:, tt * D:(tt + 1) * D])),
                                        waits=[s4s[q]])
                            lastuse[("ln2C", q)] = y2
                            lastuse[("x2C", q)] = [s4s[q], y2]
                    lastuse["a_tok"] = y2
                    finish(sch)

        if MAX_PHASE >= 4:
            with ExitStack() as php:
                PD = php.enter_context
                p_trD = [PD(nc.psum_tensor(f"p_trD{i}", [128, 1024], BF16)) for i in range(2)]
                p_xD = PD(nc.psum_tensor("p_xD", [128, 512], F32))

                for ck in range(2):
                    c0 = ck * 1024
                    with ExitStack() as ph:
                        A = ph.enter_context
                        ln2T = A(nc.sbuf_tensor(f"ln2TD{ck}", [128, DT * 1024], BF16))
                        lt = [A(nc.sbuf_tensor(f"ltD{ck}_{i}", [128, D], BF16)) for i in range(2)]
                        xlatT = A(nc.sbuf_tensor(f"xlatTD{ck}", [128, LT * 1024], BF16))
                        eoT = A(nc.sbuf_tensor(f"eoTD{ck}", [128, LT * 1024], BF16))
                        slab = [A(nc.sbuf_tensor(f"slabD{ck}_{i}", [128, D], BF16)) for i in range(2)]
                        rsb = [A(nc.sbuf_tensor(f"rsbD{ck}_{i}", [128, 1024], BF16)) for i in range(2)]
                        hsb = [A(nc.sbuf_tensor(f"hsbD{ck}_{i}", [128, 1024], BF16)) for i in range(2)]
                        rtb = [A(nc.sbuf_tensor(f"rtbD{ck}_{i}", [128, 1024], F32)) for i in range(2)]
                        lg = A(nc.sbuf_tensor(f"lgD{ck}", [128, NE], F32))
                        tw = A(nc.sbuf_tensor(f"twD{ck}", [128, 24], F32))
                        rwc = A(nc.sbuf_tensor(f"rwcD{ck}", [128, 8], F32))
                        rwr = A(nc.sbuf_tensor(f"rwrD{ck}", [1, 1024], F32))
                        fct = A(nc.sbuf_tensor(f"fctD{ck}", [128, 1024], F32))

                        ln2T_r = ln2T[:].rearrange("p (dt t) -> p dt t", dt=DT)
                        sch = new_sched()
                        lu = lastuse

                        for tt8 in range(8):
                            sel = tt8 % 2
                            ld = sch.ev("y", (lambda e, tt8=tt8, sel=sel, c0=c0:
                                              e.dma_start(lt[sel][:],
                                                          ln2s[c0 + tt8 * 128:c0 + tt8 * 128 + 128, :])),
                                        waits=[lu.get(("ltD", sel))])
                            tevs = transpose_tile(
                                sch,
                                (lambda dt, sel=sel: lt[sel][:, dt * 128:(dt + 1) * 128]),
                                DT,
                                (lambda b8, tt8=tt8: ln2T_r[:, b8 * 8:(b8 + 1) * 8,
                                                            tt8 * 128:(tt8 + 1) * 128]),
                                [ld],
                                p_trD)
                            lu[("ltD", sel)] = tevs[-1]
                            lu[("ln2T_tile", tt8)] = tevs[-1]

                            def gmm(e, tt8=tt8):
                                last = None
                                for kt in range(DT):
                                    last = e.matmul(p_xD[:, 0:8],
                                                    ln2T[:, kt * 1024 + tt8 * 128:kt * 1024 + tt8 * 128 + 128],
                                                    gate_sb[:, kt * NE:(kt + 1) * NE],
                                                    start=(kt == 0), stop=(kt == DT - 1))
                                return last
                            ge = sch.ev("t", gmm, waits=[tevs[-1], lu.get("p_xD")])

                            prev = sch.ev("v", lambda e: e.tensor_copy(lg[:], p_xD[:, 0:8]),
                                          waits=[ge])
                            lu["p_xD"] = prev
                            prev = sch.ev("v", lambda e: e.tensor_tensor(
                                tw[:, 0:1], lg[:, 0:1], lg[:, 1:2], ALU.max), waits=[prev])
                            prev = sch.ev("v", lambda e: e.tensor_tensor(
                                tw[:, 8:9], lg[:, 0:1], lg[:, 1:2], ALU.min), waits=[prev])
                            for k in range(2, 8):
                                i = k - 2
                                prev = sch.ev("v", (lambda e, i=i, k=k: e.tensor_tensor(
                                    tw[:, 16:17], tw[:, i:i + 1], lg[:, k:k + 1], ALU.min)),
                                    waits=[prev])
                                prev = sch.ev("v", (lambda e, i=i: e.tensor_tensor(
                                    tw[:, 9 + i:10 + i], tw[:, 8 + i:9 + i], tw[:, 16:17], ALU.max)),
                                    waits=[prev])
                                prev = sch.ev("v", (lambda e, i=i, k=k: e.tensor_tensor(
                                    tw[:, i + 1:i + 2], tw[:, i:i + 1], lg[:, k:k + 1], ALU.max)),
                                    waits=[prev])
                            ve = sch.ev("v", (lambda e, tt8=tt8: e.tensor_tensor(
                                rwc[:, tt8:tt8 + 1], tw[:, 6:7], tw[:, 14:15], ALU.add)),
                                waits=[prev])
                            lu[("rw", tt8)] = ve
                        for tt8 in range(8):
                            ts = sch.ev("t", (lambda e, tt8=tt8:
                                              e.transpose(p_xD[0:1, (tt8 % 4) * 128:(tt8 % 4) * 128 + 128],
                                                          rwc[:, tt8:tt8 + 1], eye_f[:])),
                                        waits=[lu[("rw", tt8)], lu.get("p_xD")])
                            if tt8 % 4 == 3:
                                vr = sch.ev("v", (lambda e, tt8=tt8:
                                                  e.tensor_copy(rwr[0:1, (tt8 // 4) * 512:(tt8 // 4) * 512 + 512],
                                                                p_xD[0:1, 0:512])),
                                            waits=[ts])
                                lu["p_xD"] = vr
                        vf = None
                        for half in range(2):
                            tf2 = sch.ev("t", (lambda e, half=half:
                                               e.matmul(p_xD[:, 0:512], ones_row[:],
                                                        rwr[0:1, half * 512:half * 512 + 512],
                                                        start=True, stop=True)),
                                         waits=[vr, lu.get("p_xD")])
                            vf = sch.ev("v", (lambda e, half=half:
                                              e.tensor_copy(fct[:, half * 512:half * 512 + 512],
                                                            p_xD[:, 0:512])),
                                        waits=[tf2])
                            lu["p_xD"] = vf

                        ln2T_done = [lu[("ln2T_tile", t8)] for t8 in range(8)]

                        def gemmD(planes, MT, KT, rhs, rhs_evs, evac_fn, slab_w):
                            me = None
                            for mt in range(MT):
                                sl = mt % 2
                                g1 = sch.ev("g", (lambda e, planes=planes, mt=mt, sl=sl, slab_w=slab_w:
                                                  e.dma_start(slab[sl][:, 0:slab_w], planes[mt, :, :])),
                                            waits=[lu.get(("slabD", sl))])
                                pp = (p_s0, p_s1)[mt % 2]

                                def mm(e, pp=pp, sl=sl, KT=KT, rhs=rhs):
                                    last = None
                                    for kt in range(KT):
                                        lhs = slab[sl][:, kt * 128:(kt + 1) * 128]
                                        for j in range(2):
                                            last = e.matmul(pp[:, j * 512:(j + 1) * 512], lhs,
                                                            rhs(kt, j), start=(kt == 0), stop=(kt == KT - 1))
                                    return last
                                me = sch.ev("t", mm, waits=[g1, lu.get(("ppD", id(pp)))] + rhs_evs)
                                lu[("slabD", sl)] = me
                                free = evac_fn(mt, pp, me)
                                lu[("ppD", id(pp))] = free
                            return me

                        def fc1_evac(mt, pp, me):
                            ve = sch.ev("v", (lambda e, mt=mt, pp=pp:
                                              e.tensor_tensor(xlatT[:, mt * 1024:(mt + 1) * 1024],
                                                              pp[:], fct[:], ALU.mult)),
                                        waits=[me, vf, lu.get("xlatT")])
                            return ve
                        gemmD(fc1_p, LT, DT,
                              (lambda kt, j, c0=c0: ln2T[:, kt * 1024 + j * 512:kt * 1024 + j * 512 + 512]),
                              ln2T_done, fc1_evac, D)
                        xlat_done = [lu[("ppD", id(p_s0))], lu[("ppD", id(p_s1))]]

                        def exp_evac(mt, pp, me):
                            se = sch.ev("s", (lambda e, mt=mt, pp=pp:
                                              e.activation(eoT[:, mt * 1024:(mt + 1) * 1024], pp[:], AF.Copy)),
                                        waits=[me, lu.get("eoT")])
                            return se
                        exp_last = gemmD(exp_p, LT, LT,
                                         (lambda kt, j: xlatT[:, kt * 1024 + j * 512:kt * 1024 + j * 512 + 512]),
                                         xlat_done, exp_evac, LAT)
                        eo_done = [lu[("ppD", id(p_s0))], lu[("ppD", id(p_s1))]]
                        lu["xlatT"] = exp_last

                        def fc2_evac(mt, pp, me):
                            sl2 = mt % 2
                            se = sch.ev("s", (lambda e, pp=pp, sl2=sl2:
                                              e.activation(rtb[sl2][:], pp[:], AF.Copy)),
                                        waits=[me, lu.get(("rtbD", sl2))])
                            ye = sch.ev("y", (lambda e, mt=mt, sl2=sl2, c0=c0:
                                              e.dma_start(routedTs[mt * 128:(mt + 1) * 128, c0:c0 + 1024],
                                                          rtb[sl2][:])),
                                        waits=[se])
                            lu[("rtbD", sl2)] = ye
                            return se
                        fc2_last = gemmD(fc2_p, DT, LT,
                                         (lambda kt, j: eoT[:, kt * 1024 + j * 512:kt * 1024 + j * 512 + 512]),
                                         eo_done, fc2_evac, LAT)
                        lu["eoT"] = fc2_last

                        def up_evac(mt, pp, me):
                            sl2 = mt % 2
                            se = sch.ev("s", (lambda e, pp=pp, sl2=sl2:
                                              e.activation(rsb[sl2][:], pp[:], AF.Relu)),
                                        waits=[me, lu.get(("rsbD", sl2))])
                            ve = sch.ev("v", (lambda e, sl2=sl2:
                                              e.tensor_mul(hsb[sl2][:], rsb[sl2][:], rsb[sl2][:])),
                                        waits=[se, lu.get(("hsbD", sl2))])
                            lu[("rsbD", sl2)] = ve
                            ye = sch.ev("y", (lambda e, mt=mt, sl2=sl2, c0=c0:
                                              e.dma_start(hTs[mt * 128:(mt + 1) * 128, c0:c0 + 1024],
                                                          hsb[sl2][:])),
                                        waits=[ve])
                            lu[("hsbD", sl2)] = ye
                            lu[("hstore", mt)] = ye
                            return se
                        up_last = gemmD(up_p, IT, DT,
                                        (lambda kt, j: ln2T[:, kt * 1024 + j * 512:kt * 1024 + j * 512 + 512]),
                                        ln2T_done, up_evac, D)
                        finish(sch)

                    with ExitStack() as ph:
                        A = ph.enter_context
                        hT = A(nc.sbuf_tensor(f"hTD{ck}", [128, IT * 1024], BF16))
                        dslab = [A(nc.sbuf_tensor(f"dslabD{ck}_{i}", [128, INT], BF16)) for i in range(2)]
                        rt = [A(nc.sbuf_tensor(f"rtD{ck}_{i}", [128, 1024], F32)) for i in range(2)]
                        fpre = [A(nc.sbuf_tensor(f"fpreD{ck}_{i}", [128, 1024], BF16)) for i in range(2)]

                        sch = new_sched()
                        lu = lastuse
                        hlds = []
                        for et in range(IT):
                            ld = sch.ev("y", (lambda e, et=et, c0=c0:
                                              e.dma_start(hT[:, et * 1024:(et + 1) * 1024],
                                                          hTs[et * 128:(et + 1) * 128, c0:c0 + 1024])),
                                        waits=[lu.get(("hstore", et))])
                            hlds.append(ld)
                        for mt in range(DT):
                            sl = mt % 2
                            g1 = sch.ev("g", (lambda e, mt=mt, sl=sl:
                                              e.dma_start(dslab[sl][:], down_p[mt, :, :])),
                                        waits=[lu.get(("dslabD", sl))])
                            lr = sch.ev("y", (lambda e, mt=mt, sl=sl, c0=c0:
                                              e.dma_start(rt[sl][:],
                                                          routedTs[mt * 128:(mt + 1) * 128, c0:c0 + 1024])),
                                        waits=[lu.get(("rtD", sl))])
                            pp = (p_s0, p_s1)[mt % 2]

                            def mm_half(e, pp, sl, half):
                                last = None
                                for kt in range(half * IT // 2, (half + 1) * IT // 2):
                                    lhs = dslab[sl][:, kt * 128:(kt + 1) * 128]
                                    for j in range(2):
                                        last = e.matmul(pp[:, j * 512:(j + 1) * 512], lhs,
                                                        hT[:, kt * 1024 + j * 512:kt * 1024 + j * 512 + 512],
                                                        start=(kt == 0), stop=(kt == IT - 1))
                                return last
                            me = sch.ev("t", (lambda e, pp=pp, sl=sl: mm_half(e, pp, sl, 0)),
                                        waits=[g1, hlds[IT // 2 - 1], lu.get(("ppD2", id(pp)))])
                            me = sch.ev("t", (lambda e, pp=pp, sl=sl: mm_half(e, pp, sl, 1)),
                                        waits=[hlds[-1]])
                            lu[("dslabD", sl)] = me
                            ve = sch.ev("v", (lambda e, pp=pp, sl=sl:
                                              e.tensor_tensor(fpre[sl][:], pp[:], rt[sl][:], ALU.add)),
                                        waits=[me, lr, lu.get(("fpreD", sl))])
                            lu[("ppD2", id(pp))] = ve
                            lu[("rtD", sl)] = ve
                            ye = sch.ev("y", (lambda e, mt=mt, sl=sl, c0=c0:
                                              e.dma_start(fpreTs[mt * 128:(mt + 1) * 128, c0:c0 + 1024],
                                                          fpre[sl][:])),
                                        waits=[ve])
                            lu[("fpreD", sl)] = ye
                        finish(sch)

        if MAX_PHASE >= 5:
            with ExitStack() as ph:
                A = ph.enter_context
                fpT = A(nc.sbuf_tensor("fpTE", [128, DT * 512], BF16))
                f_tok = A(nc.sbuf_tensor("f_tokE", [128, 4 * D], BF16))
                gamma_b = A(nc.sbuf_tensor("gammaE", [128, D], F32))
                beta_b = A(nc.sbuf_tensor("betaE", [128, D], F32))
                x2a = [A(nc.sbuf_tensor("x2aE0", [128, D], F32))] * 2
                wv2 = [A(nc.sbuf_tensor(f"wv2E{i}", [128, D], F32)) for i in range(2)]
                osb = [A(nc.sbuf_tensor(f"osbE{i}", [128, D], F32)) for i in range(2)]
                bnb = [A(nc.sbuf_tensor(f"bnbE{i}", [128, 48], F32)) for i in range(2)]
                mv = [A(nc.sbuf_tensor(f"mvE{i}", [128, 2], F32)) for i in range(2)]
                st1 = [A(nc.sbuf_tensor(f"st1E{i}", [128, 4], F32)) for i in range(2)]
                p_trE = [A(nc.psum_tensor(f"p_trE{i}", [128, 1024], BF16)) for i in range(2)]

                f_tok_r = f_tok[:].rearrange("p (tt e) -> p tt e", tt=4)

                sch0 = new_sched()
                sch0.ev("y", lambda e: e.dma_start(gamma_b[:], gamma_h))
                sch0.ev("y", lambda e: e.dma_start(beta_b[:], beta_h))
                finish(sch0)

                out_stores = []
                sch = new_sched()
                for qr in range(4):
                    q0 = qr * 512
                    lds = []
                    for et in range(DT):
                        ld = sch.ev("y", (lambda e, et=et, q0=q0:
                                          e.dma_start(fpT[:, et * 512:(et + 1) * 512],
                                                      fpreTs[et * 128:(et + 1) * 128, q0:q0 + 512])),
                                    waits=[lastuse.get("fpTE")])
                        lds.append(ld)
                    ft_done = []
                    for tt in range(4):
                        def srcf(dt, tt=tt):
                            return fpT[:, dt * 512 + tt * 128:dt * 512 + tt * 128 + 128]
                        tevs = transpose_tile(
                            sch, srcf, DT,
                            (lambda b8, tt=tt: f_tok_r[:, tt:tt + 1, :].rearrange(
                                "p o (dt q) -> p (o dt) q", dt=DT)[:, b8 * 8:(b8 + 1) * 8, :]),
                            [lds[-1], lastuse.get(("f_tokE", tt))],
                            p_trE)
                        ft_done.append(tevs[-1])
                    lastuse["fpTE"] = tevs[-1]

                    for tp in range(2):
                        v2s, s2s, g3s, v5s, s4s = [], [], [], [], []
                        row0a = q0 + tp * 256
                        ldx_a = sch.ev("y", (lambda e, row0=row0a:
                                             e.dma_start(x2a[0][:], x2s[row0:row0 + 128, :])),
                                       waits=[lastuse.get("x2aE")])
                        for q in range(2):
                            tt = tp * 2 + q
                            v2s.append(ln_stats(
                                sch, (lambda j, tt=tt:
                                      f_tok[:, tt * D + j * 512:tt * D + (j + 1) * 512]),
                                bnb[q], mv[q], st1[q], [ft_done[tt]], pkey=("spE", q)))
                        for q in range(2):
                            tt = tp * 2 + q
                            s2 = sch.ev("s", (lambda e, tt=tt, q=q:
                                              e.activation(wv2[q][:], f_tok[:, tt * D:(tt + 1) * D],
                                                           AF.Identity,
                                                           bias=st1[q][:, 3:4], scale=st1[q][:, 2:3])),
                                        waits=[v2s[q], lastuse.get(("wvE", q))])
                            lastuse[("spE", q)] = s2
                            s2s.append(s2)
                        for q in range(2):
                            tt = tp * 2 + q
                            lastuse[("f_tokE", tt)] = s2s[q]
                            g3a = sch.ev("v", (lambda e, q=q:
                                               e.tensor_mul(wv2[q][:], wv2[q][:], gamma_b[:])),
                                         waits=[s2s[q]])
                            g3 = sch.ev("v", (lambda e, q=q:
                                              e.tensor_add(wv2[q][:], wv2[q][:], beta_b[:])),
                                        waits=[g3a])
                            g3s.append(g3)
                        for q in range(2):
                            v5s.append(ln_stats(
                                sch, (lambda j, q=q: wv2[q][:, j * 512:(j + 1) * 512]),
                                bnb[q], mv[q], st1[q], [g3s[q]], pkey=("spE", q)))
                        for q in range(2):
                            s4 = sch.ev("s", (lambda e, q=q:
                                              e.activation(osb[q][:], wv2[q][:], AF.Identity,
                                                           bias=st1[q][:, 3:4], scale=st1[q][:, 2:3])),
                                        waits=[v5s[q], lastuse.get(("osbE", q))])
                            lastuse[("wvE", q)] = s4
                            lastuse[("spE", q)] = s4
                            s4s.append(s4)
                        v6a = sch.ev("v", (lambda e: e.tensor_add(osb[0][:], osb[0][:], x2a[0][:])),
                                     waits=[s4s[0], ldx_a])
                        row0b = q0 + tp * 256 + 128
                        ldx_b = sch.ev("y", (lambda e, row0=row0b:
                                             e.dma_start(x2a[1][:], x2s[row0:row0 + 128, :])),
                                       waits=[v6a])
                        v6b = sch.ev("v", (lambda e: e.tensor_add(osb[1][:], osb[1][:], x2a[1][:])),
                                     waits=[s4s[1], ldx_b])
                        lastuse["x2aE"] = v6b
                        for q, v6 in ((0, v6a), (1, v6b)):
                            row0 = q0 + tp * 256 + q * 128
                            yo = sch.ev("y", (lambda e, row0=row0, q=q:
                                              e.dma_start(y_out[row0:row0 + 128, :], osb[q][:])),
                                        waits=[v6])
                            lastuse[("osbE", q)] = yo
                            out_stores.append(yo)
                finish(sch)

                schf = new_sched()
                schf.ev("y", lambda e: e.nop(),
                        waits=[("y", out_stores[-1].cnt)], inc=False)
                finish(schf)

    return nc


_CACHE = {}


def _prep(L0_attn_w_q, L0_attn_w_k, L0_attn_w_v, L0_attn_w_o, L0_ffn_gate_w,
          L0_ffn_up_w, L0_ffn_down_w, L0_ffn_fc1_w, L0_ffn_fc2_w, L0_ffn_expert_w,
          L0_ffn_ln_gamma, L0_ffn_ln_beta):
    f32 = lambda a: np.ascontiguousarray(np.asarray(a, dtype=np.float32))

    def attn_planes(w):
        w = f32(w)
        return np.ascontiguousarray(
            w.reshape(DT, 128, DT, 128).transpose(0, 3, 2, 1).reshape(DT, 128, D))

    def ffn_planes(w, KT, MT):
        w = f32(w)
        return np.ascontiguousarray(
            w.reshape(KT, 128, MT, 128).transpose(2, 1, 0, 3).reshape(MT, 128, KT * 128))

    d = {
        "wq_p": attn_planes(L0_attn_w_q),
        "wk_p": attn_planes(L0_attn_w_k),
        "wvf_p": np.ascontiguousarray(f32(L0_attn_w_v).T.reshape(DT, 128, D)),
        "wo_p": attn_planes(L0_attn_w_o),
        "up_p": ffn_planes(L0_ffn_up_w, DT, IT),
        "down_p": ffn_planes(L0_ffn_down_w, IT, DT),
        "fc1_p": ffn_planes(L0_ffn_fc1_w, DT, LT),
        "fc2_p": ffn_planes(L0_ffn_fc2_w, LT, DT),
        "exp_p": ffn_planes(L0_ffn_expert_w, LT, LT),
        "gate_h": np.ascontiguousarray(
            f32(L0_ffn_gate_w).reshape(DT, 128, NE).transpose(1, 0, 2).reshape(128, DT * NE)),
        "gamma_h": np.ascontiguousarray(np.broadcast_to(f32(L0_ffn_ln_gamma), (128, D))),
        "beta_h": np.ascontiguousarray(np.broadcast_to(f32(L0_ffn_ln_beta), (128, D))),
        "eye_h": np.eye(128, dtype=np.float32),
        "ones_col_h": np.ones((128, 1), np.float32),
        "eps_h": np.full((128, 1), EPS, np.float32),
        "ones_row_h": np.ones((1, 128), np.float32),
    }
    return d


def kernel(L0_attn_x, L0_attn_w_q, L0_attn_w_k, L0_attn_w_v, L0_attn_w_o,
           L0_ffn_gate_w, L0_ffn_up_w, L0_ffn_down_w, L0_ffn_fc1_w, L0_ffn_fc2_w,
           L0_ffn_expert_w, L0_ffn_ln_gamma, L0_ffn_ln_beta):
    x = np.ascontiguousarray(np.asarray(L0_attn_x, dtype=np.float32))
    shared = _prep(L0_attn_w_q, L0_attn_w_k, L0_attn_w_v, L0_attn_w_o, L0_ffn_gate_w,
                   L0_ffn_up_w, L0_ffn_down_w, L0_ffn_fc1_w, L0_ffn_fc2_w,
                   L0_ffn_expert_w, L0_ffn_ln_gamma, L0_ffn_ln_beta)
    nc = build_program()
    in_maps = []
    for c in range(8):
        b, hf = c // 2, c % 2
        xb = x[b]
        x_core = np.concatenate([xb[hf * T:(hf + 1) * T], xb[(1 - hf) * T:(2 - hf) * T]], axis=0)
        m = dict(shared)
        m["x_all"] = np.ascontiguousarray(x_core)
        in_maps.append(m)
    res = run_bass_kernel_spmd(nc, in_maps, core_ids=list(range(8)))
    out = np.empty((4, 4096, 4096), np.float32)
    for c in range(8):
        b, hf = c // 2, c % 2
        out[b, hf * T:(hf + 1) * T] = np.asarray(res.results[c]["y_out"])
    return out
